# version 05
# baseline (speedup 1.0000x reference)
import contextlib
import math

import numpy as np

import concourse.bass as bass
import concourse.mybir as mybir
from concourse.bass_utils import run_bass_kernel_spmd

F32 = mybir.dt.float32
BF16 = mybir.dt.bfloat16
U8 = mybir.dt.uint8
AF = mybir.ActivationFunctionType
ALU = mybir.AluOpType
AX = mybir.AxisListType

B, S, HID = 4, 2048, 2048
NT = 16
NSLOT = 8
OWN = NSLOT * 128
LAMBDA_INIT = 0.8 - 0.6 * math.exp(-0.3 * 1.0)
CSHIFT = -14.0
DSCALE = 128.0 ** -0.5
NEGBIG = -1.0e30
N_EXP = 8
MOE_I = 1408
SH_I = 5632

STAGES = ("prologue", "stage0", "diff", "ffn", "moe", "final")
DIFF_NSP = 4
DIFF_PHASE = 9
DIFF_MEMSET = True
DIFF_NVT = 16
DIFF_OUTPROJ = True
DIFF_ATTN = True


def slot_extent(s):
    return 2 * s + 2


def core_qtiles(j):
    return list(range(j, NT, 2))


def own_rows(j):
    idx = []
    for t in core_qtiles(j):
        idx.extend(range(t * 128, (t + 1) * 128))
    return np.array(idx)


class Plan:
    ENGS = "tvsgy"

    def __init__(self):
        self.q = {k: [] for k in self.ENGS}
        self.cnt = {}
        self.lw = {}

    def c(self, sem):
        return self.cnt.get(sem, 0)

    def w(self, eng, sem, val=None):
        v = self.cnt.get(sem, 0) if val is None else val
        if v <= 0 or self.lw.get((eng, sem), 0) >= v:
            return
        self.lw[(eng, sem)] = v
        self.q[eng].append(("w", sem, v))

    def o(self, eng, fn, inc=None, k=1):
        if inc:
            self.cnt[inc] = self.cnt.get(inc, 0) + k
        self.q[eng].append(("o", fn, inc, k))
        return self.cnt.get(inc, 0) if inc else None

    def drain(self, eng):
        self.q[eng].append(("d",))

    def emit_engine(self, name, eng, sems):
        for item in self.q[name]:
            if item[0] == "w":
                eng.wait_ge(sems[item[1]], item[2])
            elif item[0] == "d":
                eng.drain()
            else:
                _, fn, inc, k = item
                ins = fn(eng)
                if inc is not None:
                    ins.then_inc(sems[inc], k)


class Arena:
    def __init__(self, nc, name, nbytes, side=None):
        kw = {"side": side} if side else {}
        self.h = nc.alloc_sbuf_tensor(name, [128, nbytes // 2], BF16, **kw)
        self.nbytes = nbytes
        self.views = {F32: self.h.bitcast(F32).ap(), BF16: self.h.ap()}

    def v(self, off, n, dt):
        es = 4 if dt == F32 else 2
        assert off % 32 == 0 and off + n * es <= self.nbytes, (off, n, dt)
        return self.views[dt][:, off // es: off // es + n]


def build_nc(taps=(), last_stage="final"):
    nc = bass.Bass("TRN2", target_bir_lowering=False)
    last_idx = STAGES.index(last_stage)
    taps = list(taps)
    build_nc.env = env = {}

    di = {}

    def dram_in(name, shape):
        di[name] = nc.dram_tensor(name, list(shape), F32, kind="ExternalInput").ap()
        return di[name]

    x_all = dram_in("x_all", [S, HID])
    x_ownd = dram_in("x_own", [OWN, HID])
    wqkv_qk = dram_in("wqkv_qk", [24, 128, 16 * 128])
    wqkv_v = dram_in("wqkv_v", [4, 128, 16 * 256])
    wqkv_b = dram_in("wqkv_b", [4096])
    outwT_t = dram_in("outwT_t", [8, 128, 16 * 256])
    outb = dram_in("outb", [HID])
    lamv = dram_in("lamv", [512])
    sublnw = dram_in("sublnw", [256])
    w_ln1 = dram_in("w_ln1", [HID])
    wq_t = dram_in("wq_t", [16, 128, 16 * 128])
    wk_t = dram_in("wk_t", [16, 128, 16 * 128])
    wv_t = dram_in("wv_t", [16, 128, 16 * 128])
    bqkv3 = dram_in("bqkv3", [3, HID])
    wo_t = dram_in("wo_t", [4, 128, 16 * 512])
    w_ln2 = dram_in("w_ln2", [HID])
    w_router_t = dram_in("w_router_t", [128, 16 * N_EXP])
    w_gate_t = dram_in("w_gate_t", [N_EXP, 11, 128, 16 * 128])
    w_up_t = dram_in("w_up_t", [N_EXP, 11, 128, 16 * 128])
    w_down_t = dram_in("w_down_t", [N_EXP, 4, 128, 11 * 512])
    w_shg_t = dram_in("w_shg_t", [44, 128, 16 * 128])
    w_shu_t = dram_in("w_shu_t", [44, 128, 16 * 128])
    w_shd_t = dram_in("w_shd_t", [16, 128, 11 * 512])
    w_sheg_t = dram_in("w_sheg_t", [128, 16])
    ident_in = dram_in("ident", [128, 128])
    sperm_in = dram_in("sperm", [128, 128])
    masks_in = dram_in("masks", [128, 256])
    ctabk = dram_in("ctabk", [4, 128, S])
    ctabq = dram_in("ctabq", [4, 128, OWN])

    y_out = nc.dram_tensor("y", [OWN, HID], F32, kind="ExternalOutput").ap()
    s_spill = nc.dram_tensor("s_spill", [OWN, HID], F32, kind="Internal").ap()
    tap_out = {}
    for stg, name, _fn, shape in taps:
        tap_out[name] = nc.dram_tensor(name, list(shape), F32, kind="ExternalOutput").ap()

    R1 = Arena(nc, "R1", 65536, side="right")
    R2 = Arena(nc, "R2", 32768, side="right")
    R3 = Arena(nc, "R3", 32768, side="right")
    R4 = Arena(nc, "R4", 16384, side="right")
    LAR = Arena(nc, "LAR", 54272)

    identb = nc.alloc_sbuf_tensor("identb", [128, 128], BF16).ap()
    identf = nc.alloc_sbuf_tensor("identf", [128, 128], F32).ap()
    spermb = nc.alloc_sbuf_tensor("spermb", [128, 128], BF16).ap()
    ones_b = nc.alloc_sbuf_tensor("ones_b", [1, 128], BF16).ap()
    masks = nc.alloc_sbuf_tensor("masks_sb", [128, 256], F32).ap()
    stat = nc.alloc_sbuf_tensor("stat", [128, 64], F32).ap()
    negC = nc.alloc_sbuf_tensor("negC", [128, 1], F32).ap()
    lamsc = nc.alloc_sbuf_tensor("lamsc", [128, 1], F32).ap()
    bstr = nc.alloc_sbuf_tensor("bstr", [128, 64], F32).ap()
    wln1s = nc.alloc_sbuf_tensor("wln1s", [128, 16], F32).ap()
    sublnb = nc.alloc_sbuf_tensor("sublnb", [128, 256], F32).ap()
    comb = nc.alloc_sbuf_tensor("comb", [128, 64], F32).ap()
    r1rowb = nc.alloc_sbuf_tensor("r1rowb", [1, 2048], BF16).ap()
    r1ownb = nc.alloc_sbuf_tensor("r1ownb", [1, 1024], BF16).ap()

    PS = [nc.alloc_psum_tensor(f"ps{i}", [128, 512], F32).ap() for i in range(6)]
    PT = nc.alloc_psum_tensor("pt", [128, 256], BF16).ap()
    PT2 = nc.alloc_psum_tensor("pt2", [128, 256], BF16).ap()
    PS5 = PS[5]

    assert nc.sbuf_bytes_remaining >= 0, nc.sbuf_bytes_remaining

    xnT_all = [R1.v(ht * 4096, 2048, BF16) for ht in range(16)]
    xnT_own = [R2.v(ht * 2048, 1024, BF16) for ht in range(16)]
    actT = [R3.v(et * 2048, 1024, BF16) for et in range(16)]
    Pb = [R4.v(kt * 1024, 512, BF16) for kt in range(16)]
    x1v = [R1.v(tt * 8192, 2048, F32) for tt in range(8)]
    h2T = [R2.v(ht * 2048, 1024, BF16) for ht in range(16)]
    guv = [R3.v(it * 2048, 1024, BF16) for it in range(11)]

    env.update(locals())
    p = Plan()
    bank_free = {}

    def bank_wait(eng, key):
        stx = bank_free.get(key)
        if stx is not None:
            p.w(eng, stx[0], stx[1])

    def bank_rel(key, sem):
        bank_free[key] = (sem, p.c(sem))

    def bank_rel_t(key, tok):
        bank_free[key] = tok

    LANES = {"xl": 3, "wl": 4, "st": 2, "sg": 2}
    lane_rr = {k: 0 for k in LANES}

    def dma_lane(qeng, prefix, fn):
        k = lane_rr[prefix]
        lane_rr[prefix] = (k + 1) % LANES[prefix]
        sem = f"{prefix}{k}"
        p.w(qeng, sem, p.c(sem))
        p.o(qeng, fn, inc=sem, k=16)
        return (sem, p.c(sem))

    def w_all(eng, prefix):
        for k in range(LANES[prefix]):
            p.w(eng, f"{prefix}{k}", None)

    def dma_y(dst, src):
        return dma_lane("y", "xl", lambda e, d=dst, s=src: e.dma_start(out=d, in_=s))

    def dma_g(dst, src):
        return dma_lane("g", "wl", lambda e, d=dst, s=src: e.dma_start(out=d, in_=s))

    def store_y(dst, src):
        return dma_lane("y", "st", lambda e, d=dst, s=src: e.dma_start(out=d, in_=s))

    def store_g(dst, src):
        return dma_lane("g", "sg", lambda e, d=dst, s=src: e.dma_start(out=d, in_=s))

    def v_op(fn, inc=None):
        return p.o("v", fn, inc=inc)

    def s_op(fn, inc=None):
        return p.o("s", fn, inc=inc)

    def t_op(fn, inc=None):
        return p.o("t", fn, inc=inc)

    def rsqrt_cols(src, dst, tmp):
        p.w("s", "ve", None)
        s_op(lambda e: e.activation(out=tmp, in_=src, func=AF.Sqrt), inc="sc")
        p.w("v", "sc", None)
        v_op(lambda e: e.reciprocal(dst, tmp))
        p.drain("v")
        v_op(lambda e: e.tensor_tensor(tmp, src, dst, ALU.mult))
        p.drain("v")
        v_op(lambda e: e.tensor_tensor(tmp, tmp, dst, ALU.mult))
        p.drain("v")
        v_op(lambda e: e.tensor_scalar(out=tmp, in0=tmp, scalar1=-0.5, scalar2=1.5,
                                       op0=ALU.mult, op1=ALU.add))
        p.drain("v")
        v_op(lambda e: e.tensor_tensor(dst, dst, tmp, ALU.mult), inc="ve")
        p.drain("v")

    def bcast_hilo(row_f32, n, hf, hb, lb, out_f32):
        p.w("s", "ve", None)
        s_op(lambda e: e.copy(out=hb[0:1, 0:n], in_=row_f32[0:1, 0:n]))
        p.drain("s")
        s_op(lambda e: e.copy(out=hf[0:1, 0:n], in_=hb[0:1, 0:n]), inc="sc")
        p.w("v", "sc", None)
        v_op(lambda e: e.tensor_tensor(hf[0:1, 0:n], row_f32[0:1, 0:n], hf[0:1, 0:n],
                                       ALU.subtract), inc="ve")
        p.w("s", "ve", None)
        s_op(lambda e: e.copy(out=lb[0:1, 0:n], in_=hf[0:1, 0:n]), inc="sc")
        for c0 in range(0, n, 512):
            cw = min(512, n - c0)
            bank_wait("t", 5)
            p.w("t", "sc", None)
            t_op(lambda e, c0=c0, cw=cw: e.matmul(PS5[:, 0:cw], ones_b[0:1, 0:128],
                                                  hb[0:1, c0:c0 + cw], start=True, stop=False))
            t_op(lambda e, c0=c0, cw=cw: e.matmul(PS5[:, 0:cw], ones_b[0:1, 0:128],
                                                  lb[0:1, c0:c0 + cw], start=False, stop=True),
                 inc="mm")
            p.w("s", "mm", None)
            s_op(lambda e, c0=c0, cw=cw: e.copy(out=out_f32[:, c0:c0 + cw], in_=PS5[:, 0:cw]),
                 inc="sc")
            bank_rel(5, "sc")

    def transpose_pair(srcA, srcB, dstA, dstB, tb_key, tb):
        bank_wait("t", tb_key)
        t_op(lambda e: e.transpose(tb[:, 0:128], srcA, identb))
        t_op(lambda e: e.transpose(tb[:, 128:256], srcB, identb), inc="mm")
        p.w("s", "mm", None)
        s_op(lambda e: e.copy(out=dstA, in_=tb[:, 0:128]))
        s_op(lambda e: e.copy(out=dstB, in_=tb[:, 128:256]), inc="sc")
        bank_rel(tb_key, "sc")

    def transpose_one(src, dst, tb_key, tb):
        bank_wait("t", tb_key)
        t_op(lambda e: e.transpose(tb[:, 0:128], src, identb), inc="mm")
        p.w("s", "mm", None)
        s_op(lambda e: e.copy(out=dst, in_=tb[:, 0:128]), inc="sc")
        bank_rel(tb_key, "sc")

    SC_TMP, SC_SSQ, SC_A, SC_RSTD, SC_T1, SC_R1ARG, SC_R1F, SC_INV, SC_Z = range(9)

    def st_col(c, w=1):
        return stat[:, c:c + w]

    def emit_taps(stage):
        for stg, name, fn, shape in taps:
            if stg != stage:
                continue
            for sem in ("mm", "ve", "sc"):
                p.w("g", sem, None)
            w_all("g", "wl")
            w_all("g", "xl")
            w_all("g", "st")
            w_all("g", "sg")
            off = 0
            for src_ap, rows in fn():
                store_g(tap_out[name][off:off + rows, :], src_ap)
                off += rows

    def phase_guard(engs=("g", "y")):
        for e in engs:
            for sem in ("mm", "ve", "sc"):
                p.w(e, sem, None)
            w_all(e, "st")
            w_all(e, "sg")

    LP_hf, LP_hb, LP_lb, LP_b32, LP_lam, LP_srow = 0, 8192, 12288, 16384, 18432, 20480
    hf_row = LAR.v(LP_hf, 2048, F32)
    hb_row = LAR.v(LP_hb, 2048, BF16)
    lb_row = LAR.v(LP_lb, 2048, BF16)
    b32 = LAR.v(LP_b32, 512, F32)
    lamrow = LAR.v(LP_lam, 512, F32)
    srow = LAR.v(LP_srow, 256, F32)

    dma_y(identf, ident_in)
    dma_y(masks, masks_in)
    dma_y(lamrow[0:1, :], lamv[None, :])
    dma_y(srow[0:1, :], sublnw[None, :])
    dma_y(b32[0:32, 0:128], wqkv_b.rearrange("(c p) -> c p", p=128))
    dma_y(b32[0:16, 128:256], bqkv3[0, :].rearrange("(c p) -> c p", p=128))
    dma_y(b32[0:16, 256:384], bqkv3[1, :].rearrange("(c p) -> c p", p=128))
    dma_y(b32[0:16, 384:512], w_ln1.rearrange("(c p) -> c p", p=128))
    dma_g(identb, ident_in)
    dma_g(spermb, sperm_in)
    v_op(lambda e: e.memset(ones_b[:, :], 1.0))
    v_op(lambda e: e.memset(negC[:, :], CSHIFT), inc="ve")

    w_all("t", "xl")
    w_all("t", "wl")
    p.w("t", "ve", None)
    for (c0, w, dst) in ((0, 32, bstr[:, 0:32]), (128, 16, bstr[:, 32:48]),
                         (256, 16, bstr[:, 48:64]), (384, 16, wln1s[:, 0:16])):
        bank_wait("t", 4)
        t_op(lambda e, c0=c0, w=w: e.transpose(PS[4][0:128, 0:w], b32[0:w, c0:c0 + 128],
                                               identf[0:w, 0:w]), inc="mm")
        p.w("s", "mm", None)
        s_op(lambda e, w=w, dst=dst: e.copy(out=dst, in_=PS[4][:, 0:w]), inc="sc")
        bank_rel(4, "sc")

    w_all("v", "xl")
    v_op(lambda e: e.tensor_tensor(hf_row[0:1, 0:128], lamrow[0:1, 0:128],
                                   lamrow[0:1, 128:256], ALU.mult))
    p.drain("v")
    v_op(lambda e: e.reduce_sum(stat[0:1, 0:1], hf_row[0:1, 0:128], axis=AX.X))
    p.drain("v")
    v_op(lambda e: e.tensor_tensor(hf_row[0:1, 0:128], lamrow[0:1, 256:384],
                                   lamrow[0:1, 384:512], ALU.mult))
    p.drain("v")
    v_op(lambda e: e.reduce_sum(stat[0:1, 1:2], hf_row[0:1, 0:128], axis=AX.X), inc="ve")
    p.w("s", "ve", None)
    s_op(lambda e: e.activation(out=stat[0:1, 2:3], in_=stat[0:1, 0:1], func=AF.Exp))
    p.drain("s")
    s_op(lambda e: e.activation(out=stat[0:1, 3:4], in_=stat[0:1, 1:2], func=AF.Exp), inc="sc")
    p.w("v", "sc", None)
    v_op(lambda e: e.tensor_tensor(stat[0:1, 4:5], stat[0:1, 3:4], stat[0:1, 2:3],
                                   ALU.subtract))
    p.drain("v")
    v_op(lambda e: e.tensor_scalar(out=stat[0:1, 4:5], in0=stat[0:1, 4:5],
                                   scalar1=-LAMBDA_INIT, scalar2=None, op0=ALU.add), inc="ve")
    p.drain("v")
    bcast_hilo(stat[0:1, 4:5], 1, hf_row, hb_row, lb_row, lamsc)
    bcast_hilo(srow, 256, hf_row, hb_row, lb_row, sublnb)
    emit_taps("prologue")

    if last_idx >= 1:
        phase_guard()
        xst = [LAR.v(0, 2048, F32), LAR.v(8192, 2048, F32)]
        xnbs = [LAR.v(16384, 2048, BF16), LAR.v(20480, 2048, BF16)]
        trash = LAR.v(24576, 2048, F32)

        for (src, ntiles, dstT, rowdst) in ((x_all, 16, xnT_all, r1rowb),
                                            (x_ownd, 8, xnT_own, r1ownb)):
            for t in range(ntiles):
                buf = xst[t % 2]
                xnb = xnbs[t % 2]
                sb = (t % 2) * 8
                bank_wait("y", f"xst{t % 2}")
                dma_y(buf, src[t * 128:(t + 1) * 128, :])
                bank_wait("v", f"stat{t % 2}")
                w_all("v", "xl")
                v_op(lambda e, c=sb, b=buf: e.reduce_sum(st_col(c + SC_TMP), b, axis=AX.X))
                p.drain("v")
                v_op(lambda e, c=sb: e.tensor_scalar(out=st_col(c + SC_TMP),
                                                     in0=st_col(c + SC_TMP),
                                                     scalar1=-1.0 / 2048, scalar2=None,
                                                     op0=ALU.mult))
                p.drain("v")
                v_op(lambda e, b=buf, c=sb: e.tensor_scalar(out=b, in0=b,
                                                            scalar1=st_col(c + SC_TMP),
                                                            scalar2=None, op0=ALU.add),
                     inc="ve")
                p.w("s", "ve", None)
                s_op(lambda e, b=buf, c=sb: e.activation(out=trash, in_=b, func=AF.Square,
                                                         accum_out=st_col(c + 1)), inc="sc")
                p.w("v", "sc", None)
                v_op(lambda e, c=sb: e.tensor_scalar(out=st_col(c + 2), in0=st_col(c + 1),
                                                     scalar1=1.0 / 2048, scalar2=1e-5,
                                                     op0=ALU.mult, op1=ALU.add))
                p.drain("v")
                v_op(lambda e, c=sb: e.reciprocal(st_col(c + 7), st_col(c + 2)))
                p.drain("v")
                v_op(lambda e, c=sb: e.tensor_scalar(out=st_col(c + 3), in0=st_col(c + 1),
                                                     scalar1=1.0 / 2048, scalar2=st_col(c + 7),
                                                     op0=ALU.mult, op1=ALU.mult))
                p.drain("v")
                v_op(lambda e, c=sb: e.tensor_scalar(out=st_col(c + 3), in0=st_col(c + 3),
                                                     scalar1=1e-6, scalar2=None, op0=ALU.add),
                     inc="ve")
                p.drain("v")
                rsqrt_cols(stat[:, sb + 2:sb + 4], stat[:, sb + 4:sb + 6],
                           stat[:, sb + 6:sb + 8])
                v_op(lambda e, b=buf, xnb=xnb, c=sb:
                     e.tensor_scalar(out=xnb, in0=b, scalar1=st_col(c + 4),
                                     scalar2=None, op0=ALU.mult), inc="ve")
                bank_rel(f"xst{t % 2}", "ve")
                p.w("t", "ve", None)
                for hp in range(8):
                    tb_key, tb = ("PT", PT) if hp % 2 == 0 else ("PT2", PT2)
                    transpose_pair(xnb[:, (2 * hp) * 128:(2 * hp + 1) * 128],
                                   xnb[:, (2 * hp + 1) * 128:(2 * hp + 2) * 128],
                                   dstT[2 * hp][:, t * 128:(t + 1) * 128],
                                   dstT[2 * hp + 1][:, t * 128:(t + 1) * 128],
                                   tb_key, tb)
                bank_wait("t", 4)
                t_op(lambda e, c=sb: e.transpose(PS[4][0:1, 0:128], st_col(c + 5), identf),
                     inc="mm")
                p.w("s", "mm", None)
                s_op(lambda e, rowdst=rowdst, t=t:
                     e.copy(out=rowdst[0:1, t * 128:(t + 1) * 128], in_=PS[4][0:1, 0:128]),
                     inc="sc")
                bank_rel(4, "sc")
                bank_rel(f"stat{t % 2}", "mm")
        emit_taps("stage0")

    def chunk_smin(ch, kt):
        for sl in range(4):
            if slot_extent(4 * ch + sl) > kt:
                return sl
        return 4

    def attn_unit(KTsrc, kbase, Qsrc, qbase, Vsrc, vstride, vw, epilogue, sc_banks=(0, 1, 4),
                  cx_banks=(2, 3, 5)):
        p.w("t", "sc", None)
        p.w("t", "ve", None)
        for ch in range(2):
            kmax = slot_extent(4 * ch + 3)
            for kt in range(kmax):
                smin = chunk_smin(ch, kt)
                qoff = smin * 128
                psk = sc_banks[kt % len(sc_banks)]
                bank_wait("t", psk)
                t_op(lambda e, kt=kt, ch=ch, qoff=qoff, psk=psk:
                     e.matmul(PS[psk][:, qoff:512],
                              KTsrc[:, kbase + kt * 128: kbase + (kt + 1) * 128],
                              Qsrc[:, qbase + ch * 512 + qoff: qbase + (ch + 1) * 512],
                              start=True, stop=True), inc="mm")
                mlist = []
                for sl in range(4):
                    gs = 4 * ch + sl
                    if kt == slot_extent(gs) - 2:
                        mlist.append((sl, 0))
                    elif kt == slot_extent(gs) - 1:
                        mlist.append((sl, 1))
                if mlist:
                    p.w("v", "mm", None)
                    for sl, m in mlist:
                        v_op(lambda e, sl=sl, m=m, psk=psk:
                             e.tensor_tensor(PS[psk][:, sl * 128:(sl + 1) * 128],
                                             PS[psk][:, sl * 128:(sl + 1) * 128],
                                             masks[:, m * 128:(m + 1) * 128], ALU.add),
                             inc="ve")
                    p.w("s", "ve", None)
                p.w("s", "mm", None)
                s_op(lambda e, kt=kt, qoff=qoff, psk=psk:
                     e.activation(out=Pb[kt][:, qoff:512], in_=PS[psk][:, qoff:512],
                                  func=AF.Exp, bias=negC[:, 0:1], scale=DSCALE), inc="sc")
                bank_rel(psk, "sc")
            p.w("t", "sc", None)
            for sl in range(4):
                gs = 4 * ch + sl
                E = slot_extent(gs)
                pck = cx_banks[sl % len(cx_banks)]
                bank_wait("t", pck)
                for kt in range(E):
                    t_op(lambda e, kt=kt, sl=sl, pck=pck, E=E:
                         e.matmul(PS[pck][:, 0:vw],
                                  Pb[kt][:, sl * 128:(sl + 1) * 128],
                                  Vsrc[:, kt * vstride: kt * vstride + vw],
                                  start=(kt == 0), stop=(kt == E - 1)),
                         inc="mm" if kt == E - 1 else None)
                epilogue(gs, pck)
        return

    if last_idx >= 2:
        phase_guard()
        VW = 257
        wregA = LAR.v(0, 2048, BF16)
        wregB = LAR.v(4096, 2048, BF16)
        wregV = LAR.v(8192, 4096, BF16)
        QTb = LAR.v(16384, 2048, BF16)
        KTb = LAR.v(20480, 4096, BF16)
        Vx = LAR.v(28672, 16 * VW, BF16)
        opair = LAR.v(37120, 2048, F32)
        vbias_row = LAR.v(45312, 1024, BF16)
        sevac = LAR.v(47360, 512, F32)
        obf = LAR.v(49408, 256, BF16)
        obrow = LAR.v(50176, 256, BF16)
        wstrip = [LAR.v(0, 4096, BF16), LAR.v(8192, 4096, BF16)]
        build_nc.env2 = {"QTb": QTb, "KTb": KTb, "Vx": Vx, "opair": opair, "wregV": wregV, "vbias_row": vbias_row, "lamsc": lamsc, "stat": stat, "s_spill": s_spill}

        vbtok = dma_g(vbias_row[0:1, :], wqkv_b[None, 3072:4096])
        for tt in (range(16) if DIFF_MEMSET else []):
            v_op(lambda e, tt=tt: e.memset(Vx[:, tt * VW + 256: tt * VW + 257], 1.0),
                 inc="ve" if tt == 15 else None)
        if not DIFF_MEMSET:
            v_op(lambda e: e.engine_nop(), inc="ve")

        def proj_ct(wreg, wtok, srcT, tcols, dst, dstbase, bias_col):
            for ch in range(tcols // 512):
                psk = ch % 2
                bank_wait("t", psk)
                p.w("t", *wtok)
                for ht in range(16):
                    t_op(lambda e, ht=ht, ch=ch, wreg=wreg, psk=psk:
                         e.matmul(PS[psk][:, 0:512], wreg[:, ht * 128:(ht + 1) * 128],
                                  srcT[ht][:, ch * 512:(ch + 1) * 512],
                                  start=(ht == 0), stop=(ht == 15)),
                         inc="mm" if ht == 15 else None)
                p.w("s", "mm", None)
                s_op(lambda e, ch=ch, psk=psk:
                     e.activation(out=dst[:, dstbase + ch * 512: dstbase + (ch + 1) * 512],
                                  in_=PS[psk][:, 0:512], func=AF.Identity,
                                  bias=bstr[:, bias_col:bias_col + 1], scale=1.0), inc="sc")
                bank_rel(psk, "sc")

        for sp in range(DIFF_NSP):
            bank_wait("g", "wregV")
            vtok = dma_g(wregV, wqkv_v[sp])
            p.w("t", "ve", None)
            for tt in (range(DIFF_NVT) if DIFF_PHASE >= 1 else ()):
                psk = tt % 2
                bank_wait("t", psk)
                p.w("t", *vtok)
                p.w("t", *vbtok)
                for ht in range(16):
                    t_op(lambda e, ht=ht, tt=tt, psk=psk:
                         e.matmul(PS[psk][:, 0:256], xnT_all[ht][:, tt * 128:(tt + 1) * 128],
                                  wregV[:, ht * 256:(ht + 1) * 256],
                                  start=(ht == 0), stop=False))
                t_op(lambda e, sp=sp, psk=psk:
                     e.matmul(PS[psk][:, 0:256], ones_b[0:1, 0:128],
                              vbias_row[0:1, sp * 256:(sp + 1) * 256],
                              start=False, stop=True), inc="mm")
                p.w("s", "mm", None)
                s_op(lambda e, tt=tt, psk=psk: e.copy(out=Vx[:, tt * VW:tt * VW + 256],
                                                      in_=PS[psk][:, 0:256]), inc="sc")
                bank_rel(psk, "sc")
            bank_rel("wregV", "mm")
            for comp in (range(2) if DIFF_PHASE >= 2 else ()):
                wreg, key = (wregA, "wregA") if comp == 0 else (wregB, "wregB")
                bank_wait("g", key)
                ct = 16 + sp * 2 + comp
                ktok = dma_g(wreg, wqkv_qk[ct])
                proj_ct(wreg, ktok, xnT_all, 2048, KTb, comp * 2048, ct)
                bank_rel(key, "mm")
            for hq in ((2 * sp, 2 * sp + 1) if DIFF_PHASE >= 3 else ()):
                for comp in range(2):
                    wreg, key = (wregA, "wregA") if comp == 0 else (wregB, "wregB")
                    bank_wait("g", key)
                    ct = hq * 2 + comp
                    qtok = dma_g(wreg, wqkv_qk[ct])
                    proj_ct(wreg, qtok, xnT_own, 1024, QTb, comp * 1024, ct)
                    bank_rel(key, "mm")

                def diff_epi(gs, pck, comp):
                    p.w("v", "mm", None)
                    p.drain("v")
                    v_op(lambda e, pck=pck: e.reciprocal(st_col(SC_Z), PS[pck][:, 256:257]))
                    p.drain("v")
                    if comp == 1:
                        v_op(lambda e: e.tensor_tensor(st_col(SC_Z), st_col(SC_Z),
                                                       lamsc[:, 0:1], ALU.mult))
                        p.drain("v")
                        v_op(lambda e, gs=gs, pck=pck:
                             e.tensor_scalar(out=opair[:, gs * 256:(gs + 1) * 256],
                                             in0=PS[pck][:, 0:256], scalar1=st_col(SC_Z),
                                             scalar2=None, op0=ALU.mult), inc="ve")
                    else:
                        v_op(lambda e, gs=gs, pck=pck:
                             e.scalar_tensor_tensor(out=opair[:, gs * 256:(gs + 1) * 256],
                                                    in0=PS[pck][:, 0:256],
                                                    scalar=st_col(SC_Z),
                                                    in1=opair[:, gs * 256:(gs + 1) * 256],
                                                    op0=ALU.mult, op1=ALU.add), inc="ve")
                    bank_rel(pck, "ve")

                if not DIFF_ATTN:
                    continue
                for comp in (1, 0):
                    attn_unit(KTb, comp * 2048, QTb, comp * 1024, Vx, VW, VW,
                              lambda gs, pck, comp=comp: diff_epi(gs, pck, comp))
                bank_rel("VxRead", "mm")
                p.w("s", "ve", None)
                for sl in range(8):
                    s_op(lambda e, sl=sl: e.activation(out=sevac[:, 0:256],
                                                       in_=opair[:, sl * 256:(sl + 1) * 256],
                                                       func=AF.Square,
                                                       accum_out=stat[:, 16 + sl:17 + sl]),
                         inc="sc")
                    p.drain("s")
                p.w("v", "sc", None)
                v_op(lambda e: e.tensor_scalar(out=stat[:, 24:32], in0=stat[:, 16:24],
                                               scalar1=1.0 / 256, scalar2=1e-5,
                                               op0=ALU.mult, op1=ALU.add), inc="ve")
                p.drain("v")
                rsqrt_cols(stat[:, 24:32], stat[:, 32:40], stat[:, 40:48])
                v_op(lambda e: e.tensor_scalar(out=stat[:, 32:40], in0=stat[:, 32:40],
                                               scalar1=(1.0 - LAMBDA_INIT), scalar2=None,
                                               op0=ALU.mult), inc="ve")
                p.drain("v")
                for sl in range(8):
                    p.w("v", "mm", None)
                    v_op(lambda e, sl=sl:
                         e.scalar_tensor_tensor(out=obf[:, 0:256],
                                                in0=opair[:, sl * 256:(sl + 1) * 256],
                                                scalar=stat[:, 32 + sl:33 + sl],
                                                in1=sublnb[:, 0:256],
                                                op0=ALU.mult, op1=ALU.mult), inc="ve")
                    p.w("t", "ve", None)
                    transpose_pair(obf[:, 0:128], obf[:, 128:256],
                                   actT[2 * hq][:, sl * 128:(sl + 1) * 128],
                                   actT[2 * hq + 1][:, sl * 128:(sl + 1) * 128],
                                   "PT" if sl % 2 == 0 else "PT2",
                                   PT if sl % 2 == 0 else PT2)

        phase_guard(("g",))
        p.w("t", "sc", None)
        if not DIFF_OUTPROJ:
            emit_taps("diff")
        for js in range(8):
            wkey = f"wstrip{js % 2}"
            bank_wait("g", wkey)
            wtok1 = dma_g(wstrip[js % 2], outwT_t[js])
            bank_wait("g", "obrow")
            wtok2 = dma_g(obrow[0:1, 0:256], outb[None, js * 256:(js + 1) * 256])
            for tt in range(8):
                psk = tt % 2
                bank_wait("t", psk)
                p.w("t", *wtok1)
                p.w("t", *wtok2)
                for et in range(16):
                    t_op(lambda e, et=et, tt=tt, js=js, psk=psk:
                         e.matmul(PS[psk][:, 0:256], actT[et][:, tt * 128:(tt + 1) * 128],
                                  wstrip[js % 2][:, et * 256:(et + 1) * 256],
                                  start=(et == 0), stop=False))
                t_op(lambda e, psk=psk:
                     e.matmul(PS[psk][:, 0:256], ones_b[0:1, 0:128],
                              obrow[0:1, 0:256],
                              start=False, stop=True), inc="mm")
                par = tt % 2
                bank_wait("s", f"sevac{par}")
                p.w("s", "mm", None)
                s_op(lambda e, par=par, psk=psk:
                     e.copy(out=sevac[:, par * 256:(par + 1) * 256], in_=PS[psk][:, 0:256]),
                     inc="sc")
                bank_rel(psk, "sc")
                p.w("y", "sc", None)
                stok = store_y(s_spill[tt * 128:(tt + 1) * 128, js * 256:(js + 1) * 256],
                               sevac[:, par * 256:(par + 1) * 256])
                bank_rel_t(f"sevac{par}", stok)
            bank_rel(wkey, "mm")
            bank_rel("obrow", "mm")
        if DIFF_OUTPROJ:
            emit_taps("diff")

    if last_idx >= 3:
        phase_guard()
        wsl = [LAR.v(i * 4096, 2048, BF16) for i in range(6)]
        cosq = LAR.v(24576, 1024, BF16)
        sinq = LAR.v(26624, 1024, BF16)
        cosk = LAR.v(28672, 2048, BF16)
        sink = LAR.v(32768, 2048, BF16)
        QTr = LAR.v(36864, 1024, BF16)
        KTr = LAR.v(38912, 2048, BF16)
        VWF = 132
        Vx2 = LAR.v(43008, 16 * VWF, BF16)
        bvrow = LAR.v(47232, 128, BF16)
        ctxbf = LAR.v(47520, 128, BF16) if False else LAR.v(47488, 128, BF16)
        rtmp = LAR.v(47744, 1024, BF16)
        tabA = LAR.v(0, 2048, F32)
        tabB = LAR.v(8192, 2048, F32)
        build_nc.env3 = {"QTr": QTr, "KTr": KTr, "Vx2": Vx2, "x1v": x1v, "cosq": cosq,
                         "sinq": sinq, "cosk": cosk, "sink": sink}
        wostrip = [LAR.v(0, 8192, BF16), LAR.v(16384, 8192, BF16)]
        xtile = LAR.v(32768, 2048, F32)
        trash2 = LAR.v(40960, 2048, F32)

        for (row, tgt, nblk) in ((r1rowb, xnT_all, 4), (r1ownb, xnT_own, 2)):
            for tb in range(nblk):
                bank_wait("t", 5)
                p.w("t", "sc", None)
                t_op(lambda e, row=row, tb=tb:
                     e.matmul(PS5[:, 0:512], ones_b[0:1, 0:128],
                              row[0:1, tb * 512:(tb + 1) * 512], start=True, stop=True),
                     inc="mm")
                p.w("v", "mm", None)
                for ht in range(16):
                    v_op(lambda e, ht=ht, tb=tb, tgt=tgt:
                         e.scalar_tensor_tensor(out=tgt[ht][:, tb * 512:(tb + 1) * 512],
                                                in0=tgt[ht][:, tb * 512:(tb + 1) * 512],
                                                scalar=wln1s[:, ht:ht + 1],
                                                in1=PS5[:, 0:512],
                                                op0=ALU.mult, op1=ALU.mult),
                         inc="ve" if ht == 15 else None)
                bank_rel(5, "ve")

        for (dst, n, hi, lo) in ((cosq, 1024, ctabq[0], ctabq[1]),
                                 (sinq, 1024, ctabq[2], ctabq[3]),
                                 (cosk, 2048, ctabk[0], ctabk[1]),
                                 (sink, 2048, ctabk[2], ctabk[3])):
            bank_wait("y", "tabAB")
            dma_y(tabA[:, 0:n], hi)
            dma_y(tabB[:, 0:n], lo)
            w_all("v", "xl")
            v_op(lambda e, dst=dst, n=n: e.tensor_tensor(dst[:, 0:n], tabA[:, 0:n],
                                                         tabB[:, 0:n], ALU.add), inc="ve")
            bank_rel("tabAB", "ve")
        phase_guard(("g",))

        def rope(raw, costab, sintab, nchunk, rot_banks=(2, 3)):
            p.w("t", "sc", None)
            p.w("t", "ve", None)
            for ch in range(nchunk):
                rb = rot_banks[ch % 2]
                bank_wait("t", rb)
                t_op(lambda e, ch=ch, raw=raw, rb=rb:
                     e.matmul(PS[rb][:, 0:512], spermb,
                              raw[:, ch * 512:(ch + 1) * 512], start=True, stop=True),
                     inc="mm")
                p.w("v", "mm", None)
                v_op(lambda e, ch=ch, raw=raw, costab=costab:
                     e.tensor_tensor(raw[:, ch * 512:(ch + 1) * 512],
                                     raw[:, ch * 512:(ch + 1) * 512],
                                     costab[:, ch * 512:(ch + 1) * 512], ALU.mult))
                v_op(lambda e, ch=ch, sintab=sintab, rb=rb:
                     e.tensor_tensor(rtmp[:, (ch % 2) * 512:(ch % 2 + 1) * 512],
                                     PS[rb][:, 0:512],
                                     sintab[:, ch * 512:(ch + 1) * 512], ALU.mult))
                p.drain("v")
                v_op(lambda e, ch=ch, raw=raw:
                     e.tensor_tensor(raw[:, ch * 512:(ch + 1) * 512],
                                     raw[:, ch * 512:(ch + 1) * 512],
                                     rtmp[:, (ch % 2) * 512:(ch % 2 + 1) * 512], ALU.add),
                     inc="ve")
                bank_rel(rb, "ve")

        for h in range(16):
            gsl = 3 * (h % 2)
            wlv = []
            for (mat, idx) in ((wq_t, 0), (wk_t, 1), (wv_t, 2)):
                bank_wait("g", f"wsl{gsl + idx}")
                wlv.append(dma_g(wsl[gsl + idx], mat[h]))
            bank_wait("g", "bvrow")
            bvtok = dma_g(bvrow[0:1, 0:128], bqkv3[2, h * 128:(h + 1) * 128][None, :])
            p.w("s", "mm", None)
            p.w("v", "mm", None)
            proj_ct(wsl[gsl + 0], wlv[0], xnT_own, 1024, QTr, 0, 32 + h)
            proj_ct(wsl[gsl + 1], wlv[1], xnT_all, 2048, KTr, 0, 48 + h)
            rope(QTr, cosq, sinq, 2)
            rope(KTr, cosk, sink, 4)
            p.w("v", "mm", None)
            v_op(lambda e: e.memset(Vx2[:, :], 1.0), inc="ve")
            for tt in range(16):
                psk = tt % 2
                bank_wait("t", psk)
                p.w("t", *wlv[2])
                p.w("t", *bvtok)
                for ht in range(16):
                    t_op(lambda e, ht=ht, tt=tt, gsl=gsl, psk=psk:
                         e.matmul(PS[psk][:, 0:128], xnT_all[ht][:, tt * 128:(tt + 1) * 128],
                                  wsl[gsl + 2][:, ht * 128:(ht + 1) * 128],
                                  start=(ht == 0), stop=False))
                t_op(lambda e, psk=psk:
                     e.matmul(PS[psk][:, 0:128], ones_b[0:1, 0:128], bvrow[0:1, 0:128],
                              start=False, stop=True), inc="mm")
                p.w("s", "ve", None)
                p.w("s", "mm", None)
                s_op(lambda e, tt=tt, psk=psk:
                     e.copy(out=Vx2[:, tt * VWF:tt * VWF + 128], in_=PS[psk][:, 0:128]),
                     inc="sc")
                bank_rel(psk, "sc")
            bank_rel(f"wsl{gsl}", "mm")
            bank_rel(f"wsl{gsl + 1}", "mm")
            bank_rel(f"wsl{gsl + 2}", "mm")
            bank_rel("bvrow", "mm")

            def ffn_epi(gs, pck, h=h):
                p.w("v", "mm", None)
                p.drain("v")
                v_op(lambda e, pck=pck: e.reciprocal(st_col(SC_Z), PS[pck][:, 128:129]))
                p.drain("v")
                v_op(lambda e, pck=pck: e.tensor_scalar(out=ctxbf[:, 0:128],
                                                        in0=PS[pck][:, 0:128],
                                                        scalar1=st_col(SC_Z), scalar2=None,
                                                        op0=ALU.mult), inc="ve")
                bank_rel(pck, "ve")
                p.w("t", "ve", None)
                transpose_one(ctxbf[:, 0:128], actT[h][:, gs * 128:(gs + 1) * 128],
                              "PT" if gs % 2 == 0 else "PT2", PT if gs % 2 == 0 else PT2)

            attn_unit(KTr, 0, QTr, 0, Vx2, VWF, 129, ffn_epi)

        phase_guard(("g", "y"))
        for js in range(4):
            wkey = f"wostrip{js % 2}"
            bank_wait("g", wkey)
            wotok = dma_g(wostrip[js % 2], wo_t[js])
            for tt in range(8):
                psk = tt % 2
                bank_wait("t", psk)
                p.w("t", *wotok)
                for et in range(16):
                    t_op(lambda e, et=et, tt=tt, js=js, psk=psk:
                         e.matmul(PS[psk][:, 0:512], actT[et][:, tt * 128:(tt + 1) * 128],
                                  wostrip[js % 2][:, et * 512:(et + 1) * 512],
                                  start=(et == 0), stop=(et == 15)), inc="mm")
                p.w("v", "mm", None)
                v_op(lambda e, tt=tt, js=js, psk=psk:
                     e.tensor_copy(out=x1v[tt][:, js * 512:(js + 1) * 512],
                                   in_=PS[psk][:, 0:512]), inc="ve")
                bank_rel(psk, "ve")
            bank_rel(wkey, "mm")
        for tt in range(8):
            p.w("y", "ve", None)
            dma_y(xtile, x_ownd[tt * 128:(tt + 1) * 128, :])
            w_all("v", "xl")
            v_op(lambda e: e.reduce_sum(st_col(SC_TMP), xtile, axis=AX.X))
            p.drain("v")
            v_op(lambda e: e.tensor_scalar(out=st_col(SC_TMP), in0=st_col(SC_TMP),
                                           scalar1=-1.0 / 2048, scalar2=None, op0=ALU.mult))
            p.drain("v")
            v_op(lambda e: e.tensor_scalar(out=xtile, in0=xtile, scalar1=st_col(SC_TMP),
                                           scalar2=None, op0=ALU.add), inc="ve")
            p.w("s", "ve", None)
            s_op(lambda e: e.activation(out=trash2, in_=xtile, func=AF.Square,
                                        accum_out=st_col(SC_SSQ)), inc="sc")
            p.w("v", "sc", None)
            v_op(lambda e: e.tensor_scalar(out=st_col(SC_A), in0=st_col(SC_SSQ),
                                           scalar1=1.0 / 2048, scalar2=1e-5,
                                           op0=ALU.mult, op1=ALU.add), inc="ve")
            p.drain("v")
            rsqrt_cols(st_col(SC_A), st_col(SC_RSTD), st_col(SC_T1))
            v_op(lambda e: e.tensor_scalar(out=xtile, in0=xtile, scalar1=st_col(SC_RSTD),
                                           scalar2=None, op0=ALU.mult))
            p.drain("v")
            v_op(lambda e, tt=tt: e.tensor_tensor(x1v[tt], x1v[tt], xtile, ALU.add),
                 inc="ve")
        emit_taps("ffn")

    if last_idx >= 4:
        phase_guard()
        wln2row = LAR.v(0, 2048, F32)
        hfm = LAR.v(8192, 2048, F32)
        hbm = LAR.v(16384, 2048, BF16)
        lbm = LAR.v(20480, 2048, BF16)
        wln2b = LAR.v(24576, 2048, F32)
        h2stg = LAR.v(32768, 2048, BF16)
        trash3 = LAR.v(36864, 2048, F32)
        wrouterb = LAR.v(45056, 128, BF16)

        dma_y(wln2row[0:1, :], w_ln2[None, :])
        w_all("v", "xl")
        v_op(lambda e: e.engine_nop(), inc="ve")
        bcast_hilo(wln2row, 2048, hfm, hbm, lbm, wln2b)
        p.w("v", "sc", None)
        for tt in range(8):
            p.w("s", "ve", None)
            s_op(lambda e, tt=tt: e.activation(out=trash3, in_=x1v[tt], func=AF.Square,
                                               accum_out=st_col(SC_SSQ)), inc="sc")
            p.w("v", "sc", None)
            v_op(lambda e: e.tensor_scalar(out=st_col(SC_A), in0=st_col(SC_SSQ),
                                           scalar1=1.0 / 2048, scalar2=1e-6,
                                           op0=ALU.mult, op1=ALU.add), inc="ve")
            p.drain("v")
            rsqrt_cols(st_col(SC_A), st_col(SC_RSTD), st_col(SC_T1))
            p.w("v", "mm", None)
            v_op(lambda e, tt=tt: e.scalar_tensor_tensor(out=h2stg, in0=x1v[tt],
                                                         scalar=st_col(SC_RSTD),
                                                         in1=wln2b, op0=ALU.mult,
                                                         op1=ALU.mult), inc="ve")
            p.w("t", "ve", None)
            for hp in range(8):
                tb_key, tb = ("PT", PT) if hp % 2 == 0 else ("PT2", PT2)
                transpose_pair(h2stg[:, (2 * hp) * 128:(2 * hp + 1) * 128],
                               h2stg[:, (2 * hp + 1) * 128:(2 * hp + 2) * 128],
                               h2T[2 * hp][:, tt * 128:(tt + 1) * 128],
                               h2T[2 * hp + 1][:, tt * 128:(tt + 1) * 128], tb_key, tb)

        phase_guard(("g",))
        rtok = dma_g(wrouterb[:, 0:128], w_router_t)
        p.w("t", "sc", None)
        for tt in range(8):
            psk = tt % 2
            bank_wait("t", psk)
            p.w("t", *rtok)
            for ht in range(16):
                t_op(lambda e, ht=ht, tt=tt, psk=psk:
                     e.matmul(PS[psk][:, 0:8], h2T[ht][:, tt * 128:(tt + 1) * 128],
                              wrouterb[:, ht * 8:(ht + 1) * 8],
                              start=(ht == 0), stop=(ht == 15)), inc="mm")
            p.w("v", "mm", None)
            v_op(lambda e, psk=psk: e.reduce_max(st_col(SC_TMP), PS[psk][:, 0:8],
                                                 axis=AX.X, negate=True), inc="ve")
            p.drain("v")
            p.w("s", "ve", None)
            s_op(lambda e, psk=psk: e.activation(out=stat[:, 40:48], in_=PS[psk][:, 0:8],
                                                 func=AF.Exp, bias=st_col(SC_TMP),
                                                 scale=1.0), inc="sc")
            bank_rel(psk, "sc")
            p.w("v", "sc", None)
            v_op(lambda e: e.reduce_sum(st_col(SC_TMP), stat[:, 40:48], axis=AX.X))
            p.drain("v")
            v_op(lambda e: e.reciprocal(st_col(SC_A), st_col(SC_TMP)))
            p.drain("v")
            v_op(lambda e: e.tensor_scalar(out=stat[:, 40:48], in0=stat[:, 40:48],
                                           scalar1=st_col(SC_A), scalar2=None,
                                           op0=ALU.mult))
            p.drain("v")
            v_op(lambda e: e.reduce_max(st_col(SC_TMP), stat[:, 40:48], axis=AX.X))
            p.drain("v")
            v_op(lambda e: e.tensor_scalar(out=stat[:, 48:56], in0=stat[:, 40:48],
                                           scalar1=st_col(SC_TMP), scalar2=None,
                                           op0=ALU.is_equal))
            p.drain("v")
            v_op(lambda e: e.tensor_scalar(out=stat[:, 48:56], in0=stat[:, 48:56],
                                           scalar1=-1.0, scalar2=1.0,
                                           op0=ALU.mult, op1=ALU.add))
            p.drain("v")
            v_op(lambda e: e.tensor_tensor(stat[:, 48:56], stat[:, 40:48], stat[:, 48:56],
                                           ALU.mult))
            p.drain("v")
            v_op(lambda e: e.reduce_max(st_col(SC_TMP), stat[:, 48:56], axis=AX.X))
            p.drain("v")
            v_op(lambda e: e.tensor_scalar(out=stat[:, 48:56], in0=stat[:, 40:48],
                                           scalar1=st_col(SC_TMP), scalar2=None,
                                           op0=ALU.is_ge))
            p.drain("v")
            v_op(lambda e, tt=tt: e.tensor_tensor(comb[:, tt * 8:(tt + 1) * 8],
                                                  stat[:, 40:48], stat[:, 48:56],
                                                  ALU.mult), inc="ve")
            p.drain("v")
        bank_rel("shegb", "mm")
        phase_guard(("g",))

        wgbuf = [LAR.v(0, 2048, BF16), LAR.v(4096, 2048, BF16)]
        wubuf = [LAR.v(8192, 2048, BF16), LAR.v(12288, 2048, BF16)]
        wdbuf = [LAR.v(16384, 5632, BF16), LAR.v(27648, 5632, BF16)]
        cbrow = LAR.v(38912, 1024, F32)
        cbhf = LAR.v(43008, 1024, F32)
        cbhb = LAR.v(47104, 1024, BF16)
        cblb = LAR.v(49152, 1024, BF16)
        cbb = LAR.v(51200, 1024, BF16)
        gsil = LAR.v(43008, 512, BF16)
        gtmp = LAR.v(44032, 512, BF16)

        def run_expert(gsrc, usrc, dsrc, nit):
            strips = [(i, 1) for i in range(nit)]
            p.w("v", "mm", None)
            for si, (i0, iw) in enumerate(strips):
                wtoks = []
                for (bufs, srcf) in ((wgbuf, gsrc), (wubuf, usrc)):
                    key = f"{'wg' if bufs is wgbuf else 'wu'}{si % 2}"
                    bank_wait("g", key)
                    wtoks.append(dma_g(bufs[si % 2][:, 0:16 * iw * 128], srcf(i0, iw)))
                for li in range(iw):
                    it = i0 + li
                    for half in range(2):
                        pg, pu = half, 2 + half
                        bank_wait("t", pg)
                        bank_wait("t", pu)
                        p.w("t", *wtoks[0])
                        p.w("t", *wtoks[1])
                        for ht in range(16):
                            t_op(lambda e, ht=ht, li=li, half=half, si=si, iw=iw, pg=pg:
                                 e.matmul(PS[pg][:, 0:512],
                                          wgbuf[si % 2][:, ht * iw * 128 + li * 128:
                                                        ht * iw * 128 + (li + 1) * 128],
                                          h2T[ht][:, half * 512:(half + 1) * 512],
                                          start=(ht == 0), stop=(ht == 15)))
                        for ht in range(16):
                            t_op(lambda e, ht=ht, li=li, half=half, si=si, iw=iw, pu=pu:
                                 e.matmul(PS[pu][:, 0:512],
                                          wubuf[si % 2][:, ht * iw * 128 + li * 128:
                                                        ht * iw * 128 + (li + 1) * 128],
                                          h2T[ht][:, half * 512:(half + 1) * 512],
                                          start=(ht == 0), stop=(ht == 15)),
                                 inc="mm" if ht == 15 else None)
                        p.w("s", "mm", None)
                        p.w("s", "ve", None)
                        s_op(lambda e, pg=pg: e.activation(out=gsil[:, 0:512],
                                                           in_=PS[pg][:, 0:512],
                                                           func=AF.Silu), inc="sc")
                        bank_rel(pg, "sc")
                        p.w("v", "sc", None)
                        v_op(lambda e, pu=pu: e.tensor_tensor(gtmp[:, 0:512],
                                                              PS[pu][:, 0:512],
                                                              gsil[:, 0:512], ALU.mult))
                        p.drain("v")
                        v_op(lambda e, it=it, half=half:
                             e.tensor_tensor(guv[it][:, half * 512:(half + 1) * 512],
                                             gtmp[:, 0:512],
                                             cbb[:, half * 512:(half + 1) * 512],
                                             ALU.mult), inc="ve")
                        bank_rel(pu, "ve")
                bank_rel(f"wg{si % 2}", "mm")
                bank_rel(f"wu{si % 2}", "mm")
            for js in range(4):
                key = f"wd{js % 2}"
                bank_wait("g", key)
                dtok = dma_g(wdbuf[js % 2][:, 0:nit * 512], dsrc(js))
                p.w("t", "ve", None)
                for tt in range(8):
                    psk = tt % 2
                    bank_wait("t", psk)
                    p.w("t", *dtok)
                    for it in range(nit):
                        t_op(lambda e, it=it, tt=tt, js=js, psk=psk:
                             e.matmul(PS[psk][:, 0:512], guv[it][:, tt * 128:(tt + 1) * 128],
                                      wdbuf[js % 2][:, it * 512:(it + 1) * 512],
                                      start=(it == 0), stop=(it == nit - 1)),
                             inc="mm" if it == nit - 1 else None)
                    p.w("v", "mm", None)
                    v_op(lambda e, tt=tt, js=js, psk=psk:
                         e.tensor_tensor(x1v[tt][:, js * 512:(js + 1) * 512],
                                         x1v[tt][:, js * 512:(js + 1) * 512],
                                         PS[psk][:, 0:512], ALU.add), inc="ve")
                    bank_rel(psk, "ve")
                bank_rel(key, "mm")

        def build_cbb_from_statcols(col_of_tt):
            for tt in range(8):
                bank_wait("t", 4)
                p.w("t", "ve", None)
                t_op(lambda e, tt=tt: e.transpose(PS[4][0:1, 0:128], col_of_tt(tt), identf),
                     inc="mm")
                p.w("s", "mm", None)
                s_op(lambda e, tt=tt: e.copy(out=cbrow[0:1, tt * 128:(tt + 1) * 128],
                                             in_=PS[4][0:1, 0:128]), inc="sc")
                bank_rel(4, "sc")
            p.w("v", "sc", None)
            v_op(lambda e: e.engine_nop(), inc="ve")
            bcast_hilo(cbrow, 1024, cbhf, cbhb, cblb, cbb)
            p.w("v", "sc", None)
            v_op(lambda e: e.engine_nop(), inc="ve")

        for ex in range(N_EXP):
            build_cbb_from_statcols(lambda tt, ex=ex: comb[:, tt * 8 + ex: tt * 8 + ex + 1])
            run_expert(
                lambda i0, iw, ex=ex: w_gate_t[ex, i0],
                lambda i0, iw, ex=ex: w_up_t[ex, i0],
                lambda js, ex=ex: w_down_t[ex, js],
                11)

        shegb = LAR.v(45056, 16, BF16)
        bank_wait("g", "shegb")
        egtok = dma_g(shegb[:, 0:16], w_sheg_t)
        bank_wait("t", 0)
        p.w("t", *egtok)
        for tt in range(8):
            for ht in range(16):
                t_op(lambda e, ht=ht, tt=tt:
                     e.matmul(PS[0][:, tt:tt + 1], h2T[ht][:, tt * 128:(tt + 1) * 128],
                              shegb[:, ht:ht + 1], start=(ht == 0), stop=(ht == 15)),
                     inc="mm" if (ht == 15 and tt == 7) else None)
        p.w("s", "mm", None)
        s_op(lambda e: e.activation(out=stat[:, 56:64], in_=PS[0][:, 0:8],
                                    func=AF.Sigmoid), inc="sc")
        bank_rel(0, "sc")
        p.w("v", "sc", None)
        v_op(lambda e: e.engine_nop(), inc="ve")
        build_cbb_from_statcols(lambda tt: stat[:, 56 + tt:57 + tt])
        for chk in range(4):
            run_expert(
                lambda i0, iw, chk=chk: w_shg_t[chk * 11 + i0],
                lambda i0, iw, chk=chk: w_shu_t[chk * 11 + i0],
                lambda js, chk=chk: w_shd_t[chk * 4 + js],
                11)
        emit_taps("moe")

    if last_idx >= 5:
        phase_guard()
        gt = [LAR.v(0, 2048, F32), LAR.v(8192, 2048, F32)]
        xt2 = LAR.v(16384, 2048, F32)
        st2 = [LAR.v(24576, 2048, F32), LAR.v(32768, 2048, F32)]
        for tt in range(8):
            bank_wait("y", f"st2{tt % 2}")
            dma_y(st2[tt % 2], s_spill[tt * 128:(tt + 1) * 128, :])
            bank_wait("y", "xt2")
            dma_y(xt2, x_ownd[tt * 128:(tt + 1) * 128, :])
            bank_wait("s", f"gt{tt % 2}")
            p.w("s", "ve", None)
            s_op(lambda e, tt=tt: e.activation(out=gt[tt % 2], in_=x1v[tt], func=AF.Silu),
                 inc="sc")
            p.w("v", "sc", None)
            w_all("v", "xl")
            v_op(lambda e, tt=tt: e.tensor_tensor(gt[tt % 2], st2[tt % 2], gt[tt % 2],
                                                  ALU.mult))
            bank_rel(f"st2{tt % 2}", "ve")
            p.drain("v")
            v_op(lambda e, tt=tt: e.tensor_tensor(gt[tt % 2], gt[tt % 2], xt2, ALU.add),
                 inc="ve")
            bank_rel("xt2", "ve")
            p.w("y", "ve", None)
            ytok = store_y(y_out[tt * 128:(tt + 1) * 128, :], gt[tt % 2])
            bank_rel_t(f"gt{tt % 2}", ytok)
        emit_taps("final")

    w_all("y", "st")
    w_all("y", "sg")
    w_all("g", "st")
    w_all("g", "sg")

    sem_names = (["mm", "ve", "sc"] + [f"xl{i}" for i in range(3)]
                 + [f"wl{i}" for i in range(4)] + [f"st{i}" for i in range(2)]
                 + [f"sg{i}" for i in range(2)])
    with contextlib.ExitStack() as stk:
        sems = {n: stk.enter_context(nc.semaphore(n)) for n in sem_names}
        blk = stk.enter_context(nc.Block())
        for name, deco in (("t", blk.tensor), ("v", blk.vector), ("s", blk.scalar),
                           ("g", blk.gpsimd), ("y", blk.sync)):
            deco(lambda eng, name=name: p.emit_engine(name, eng, sems))
    return nc


def make_host_constants():
    ident = np.eye(128, dtype=np.float32)
    sperm = np.zeros((128, 128), np.float32)
    for dp in range(64):
        sperm[dp + 64, dp] = -1.0
    for dp in range(64, 128):
        sperm[dp - 64, dp] = 1.0
    import jax
    import jax.numpy as jnp
    with jax.default_device(jax.devices("cpu")[0]):
        invj = 1.0 / 1.0e6 ** (jnp.arange(0, 128, 2, dtype=jnp.float32) / 128)
        freqs = jnp.arange(S, dtype=jnp.float32)[:, None] * invj[None, :]
        embj = jnp.concatenate([freqs, freqs], axis=-1)
        cosj = np.asarray(jnp.cos(embj)).T
        sinj = np.asarray(jnp.sin(embj)).T
    zeros = np.zeros_like(cosj)
    ctabk = np.stack([cosj, zeros, sinj, zeros]).astype(np.float32)
    cos64, sin64 = cosj.astype(np.float64), sinj.astype(np.float64)
    return ident, sperm, ctabk, cos64, sin64


def make_masks(j):
    tri = np.where(np.arange(128)[:, None] > np.arange(128)[None, :], NEGBIG, 0.0)
    full = np.full((128, 128), NEGBIG)
    zero = np.zeros((128, 128))
    if j == 0:
        m0, m1 = tri, full
    else:
        m0, m1 = zero, tri
    return np.concatenate([m0, m1], 1).astype(np.float32)


def make_in_maps(inputs):
    x = np.ascontiguousarray(np.asarray(inputs["L0_attn_hidden_states"], np.float32))
    wqkvT = np.ascontiguousarray(np.asarray(inputs["L0_attn_Wqkv_w"], np.float32).T)
    outwT = np.ascontiguousarray(np.asarray(inputs["L0_attn_out_w"], np.float32).T)
    bqkv3 = np.ascontiguousarray(np.stack([
        np.asarray(inputs["L0_ffn_bq"], np.float32),
        np.asarray(inputs["L0_ffn_bk"], np.float32),
        np.asarray(inputs["L0_ffn_bv"], np.float32)]))
    lamv = np.concatenate([
        np.asarray(inputs["L0_attn_lambda_q1"], np.float32),
        np.asarray(inputs["L0_attn_lambda_k1"], np.float32),
        np.asarray(inputs["L0_attn_lambda_q2"], np.float32),
        np.asarray(inputs["L0_attn_lambda_k2"], np.float32)])
    ident, sperm, ctabk, cos64, sin64 = make_host_constants()

    A = np.ascontiguousarray

    def tile_in_out(w, cw):
        nin, nout = w.shape
        return A(w.reshape(nin // 128, 128, nout // cw, cw).transpose(2, 1, 0, 3)
                 .reshape(nout // cw, 128, (nin // 128) * cw))

    wq32 = np.asarray(inputs["L0_ffn_wq"], np.float32)
    wk32 = np.asarray(inputs["L0_ffn_wk"], np.float32)
    wv32 = np.asarray(inputs["L0_ffn_wv"], np.float32)
    wo32 = np.asarray(inputs["L0_ffn_wo"], np.float32)
    wg32 = np.asarray(inputs["L0_ffn_w_gate_e"], np.float32)
    wu32 = np.asarray(inputs["L0_ffn_w_up_e"], np.float32)
    wd32 = np.asarray(inputs["L0_ffn_w_down_e"], np.float32)
    shg32 = np.asarray(inputs["L0_ffn_w_sh_gate"], np.float32)
    shu32 = np.asarray(inputs["L0_ffn_w_sh_up"], np.float32)
    shd32 = np.asarray(inputs["L0_ffn_w_sh_down"], np.float32)

    common = {
        "wqkv_qk": tile_in_out(wqkvT[:, 0:3072], 128),
        "wqkv_v": tile_in_out(wqkvT[:, 3072:4096], 256),
        "wqkv_b": np.asarray(inputs["L0_attn_Wqkv_b"], np.float32),
        "outwT_t": tile_in_out(outwT, 256),
        "outb": np.asarray(inputs["L0_attn_out_b"], np.float32),
        "lamv": lamv,
        "sublnw": np.asarray(inputs["L0_attn_subln_w"], np.float32),
        "w_ln1": np.asarray(inputs["L0_ffn_w_ln1"], np.float32),
        "wq_t": tile_in_out(wq32, 128),
        "wk_t": tile_in_out(wk32, 128),
        "wv_t": tile_in_out(wv32, 128),
        "bqkv3": bqkv3,
        "wo_t": tile_in_out(wo32, 512),
        "w_ln2": np.asarray(inputs["L0_ffn_w_ln2"], np.float32),
        "w_router_t": A(np.asarray(inputs["L0_ffn_w_router"], np.float32)
                        .reshape(16, 128, 8).transpose(1, 0, 2).reshape(128, 128)),
        "w_gate_t": np.stack([tile_in_out(wg32[e], 128) for e in range(N_EXP)]),
        "w_up_t": np.stack([tile_in_out(wu32[e], 128) for e in range(N_EXP)]),
        "w_down_t": np.stack([tile_in_out(wd32[e], 512) for e in range(N_EXP)]),
        "w_shg_t": tile_in_out(shg32, 128),
        "w_shu_t": tile_in_out(shu32, 128),
        "w_shd_t": np.stack([tile_in_out(shd32[c * MOE_I:(c + 1) * MOE_I, :], 512)
                             for c in range(4)]).reshape(16, 128, 11 * 512),
        "w_sheg_t": A(np.asarray(inputs["L0_ffn_w_sh_eg"], np.float32)
                      .reshape(16, 128).T),
        "ident": ident,
        "sperm": sperm,
        "ctabk": ctabk,
    }

    in_maps = []
    for c in range(8):
        b, j = c // 2, c % 2
        rows = own_rows(j)
        cq64, sq64 = cos64[:, rows], sin64[:, rows]

        def hilo(a64):
            hi = a64.astype(np.float32)
            lo = (a64 - hi.astype(np.float64)).astype(np.float32)
            return hi, lo

        cq_hi, cq_lo = hilo(cq64)
        sq_hi, sq_lo = hilo(sq64)
        m = dict(common)
        m["x_all"] = np.ascontiguousarray(x[b])
        m["x_own"] = np.ascontiguousarray(x[b][rows])
        m["masks"] = make_masks(j)
        m["ctabq"] = np.stack([cq_hi, cq_lo, sq_hi, sq_lo]).astype(np.float32)
        in_maps.append(m)
    return in_maps


def kernel(**inputs):
    in_maps = make_in_maps(inputs)
    nc = build_nc()
    res = run_bass_kernel_spmd(nc, in_maps, core_ids=list(range(8)))
    out = np.zeros((B, S, HID), np.float32)
    for c in range(8):
        out[c // 2, own_rows(c % 2)] = res.results[c]["y"]
    return out


# version 06
# speedup vs baseline: 1.0111x; 1.0111x over previous
import contextlib
import math

import numpy as np

import concourse.bass as bass
import concourse.mybir as mybir
from concourse.bass_utils import run_bass_kernel_spmd

F32 = mybir.dt.float32
BF16 = mybir.dt.bfloat16
U8 = mybir.dt.uint8
AF = mybir.ActivationFunctionType
ALU = mybir.AluOpType
AX = mybir.AxisListType

B, S, HID = 4, 2048, 2048
NT = 16
NSLOT = 8
OWN = NSLOT * 128
LAMBDA_INIT = 0.8 - 0.6 * math.exp(-0.3 * 1.0)
CSHIFT = -14.0
DSCALE = 128.0 ** -0.5
NEGBIG = -1.0e30
N_EXP = 8
MOE_I = 1408
SH_I = 5632

STAGES = ("prologue", "stage0", "diff", "ffn", "moe", "final")
DIFF_NSP = 4
DIFF_PHASE = 9
DIFF_MEMSET = True
DIFF_NVT = 16
DIFF_OUTPROJ = True
DIFF_ATTN = True


def slot_extent(s):
    return 2 * s + 2


def core_qtiles(j):
    return list(range(j, NT, 2))


def own_rows(j):
    idx = []
    for t in core_qtiles(j):
        idx.extend(range(t * 128, (t + 1) * 128))
    return np.array(idx)


class Plan:
    ENGS = "tvsgy"

    def __init__(self):
        self.q = {k: [] for k in self.ENGS}
        self.cnt = {}
        self.lw = {}

    def c(self, sem):
        return self.cnt.get(sem, 0)

    def w(self, eng, sem, val=None):
        v = self.cnt.get(sem, 0) if val is None else val
        if v <= 0 or self.lw.get((eng, sem), 0) >= v:
            return
        self.lw[(eng, sem)] = v
        self.q[eng].append(("w", sem, v))

    def o(self, eng, fn, inc=None, k=1):
        if inc:
            self.cnt[inc] = self.cnt.get(inc, 0) + k
        self.q[eng].append(("o", fn, inc, k))
        return self.cnt.get(inc, 0) if inc else None

    def drain(self, eng):
        self.q[eng].append(("d",))

    def emit_engine(self, name, eng, sems):
        for item in self.q[name]:
            if item[0] == "w":
                eng.wait_ge(sems[item[1]], item[2])
            elif item[0] == "d":
                eng.drain()
            else:
                _, fn, inc, k = item
                ins = fn(eng)
                if inc is not None:
                    ins.then_inc(sems[inc], k)


class Arena:
    def __init__(self, nc, name, nbytes, side=None):
        kw = {"side": side} if side else {}
        self.h = nc.alloc_sbuf_tensor(name, [128, nbytes // 2], BF16, **kw)
        self.nbytes = nbytes
        self.views = {F32: self.h.bitcast(F32).ap(), BF16: self.h.ap()}

    def v(self, off, n, dt):
        es = 4 if dt == F32 else 2
        assert off % 32 == 0 and off + n * es <= self.nbytes, (off, n, dt)
        return self.views[dt][:, off // es: off // es + n]


def build_nc(taps=(), last_stage="final"):
    nc = bass.Bass("TRN2", target_bir_lowering=False)
    last_idx = STAGES.index(last_stage)
    taps = list(taps)
    build_nc.env = env = {}

    di = {}

    def dram_in(name, shape):
        di[name] = nc.dram_tensor(name, list(shape), F32, kind="ExternalInput").ap()
        return di[name]

    x_all = dram_in("x_all", [S, HID])
    x_ownd = dram_in("x_own", [OWN, HID])
    wqkv_qk = dram_in("wqkv_qk", [24, 128, 16 * 128])
    wqkv_v = dram_in("wqkv_v", [4, 128, 16 * 256])
    wqkv_b = dram_in("wqkv_b", [4096])
    outwT_t = dram_in("outwT_t", [8, 128, 16 * 256])
    outb = dram_in("outb", [HID])
    lamv = dram_in("lamv", [512])
    sublnw = dram_in("sublnw", [256])
    w_ln1 = dram_in("w_ln1", [HID])
    wq_t = dram_in("wq_t", [16, 128, 16 * 128])
    wk_t = dram_in("wk_t", [16, 128, 16 * 128])
    wv_t = dram_in("wv_t", [16, 128, 16 * 128])
    bqkv3 = dram_in("bqkv3", [3, HID])
    wo_t = dram_in("wo_t", [4, 128, 16 * 512])
    w_ln2 = dram_in("w_ln2", [HID])
    w_router_t = dram_in("w_router_t", [128, 16 * N_EXP])
    w_gate_t = dram_in("w_gate_t", [N_EXP, 11, 128, 16 * 128])
    w_up_t = dram_in("w_up_t", [N_EXP, 11, 128, 16 * 128])
    w_down_t = dram_in("w_down_t", [N_EXP, 4, 128, 11 * 512])
    w_shg_t = dram_in("w_shg_t", [44, 128, 16 * 128])
    w_shu_t = dram_in("w_shu_t", [44, 128, 16 * 128])
    w_shd_t = dram_in("w_shd_t", [16, 128, 11 * 512])
    w_sheg_t = dram_in("w_sheg_t", [128, 16])
    ident_in = dram_in("ident", [128, 128])
    sperm_in = dram_in("sperm", [128, 128])
    masks_in = dram_in("masks", [128, 256])
    ctabk = dram_in("ctabk", [4, 128, S])
    ctabq = dram_in("ctabq", [4, 128, OWN])

    y_out = nc.dram_tensor("y", [OWN, HID], F32, kind="ExternalOutput").ap()
    s_spill = nc.dram_tensor("s_spill", [OWN, HID], F32, kind="Internal").ap()
    tap_out = {}
    for stg, name, _fn, shape in taps:
        tap_out[name] = nc.dram_tensor(name, list(shape), F32, kind="ExternalOutput").ap()

    R1 = Arena(nc, "R1", 65536, side="right")
    R2 = Arena(nc, "R2", 32768, side="right")
    R3 = Arena(nc, "R3", 32768, side="right")
    R4 = Arena(nc, "R4", 16384, side="right")
    LAR = Arena(nc, "LAR", 54272)

    identb = nc.alloc_sbuf_tensor("identb", [128, 128], BF16).ap()
    identf = nc.alloc_sbuf_tensor("identf", [128, 128], F32).ap()
    spermb = nc.alloc_sbuf_tensor("spermb", [128, 128], BF16).ap()
    ones_b = nc.alloc_sbuf_tensor("ones_b", [1, 128], BF16).ap()
    masks = nc.alloc_sbuf_tensor("masks_sb", [128, 256], F32).ap()
    stat = nc.alloc_sbuf_tensor("stat", [128, 64], F32).ap()
    negC = nc.alloc_sbuf_tensor("negC", [128, 1], F32).ap()
    lamsc = nc.alloc_sbuf_tensor("lamsc", [128, 1], F32).ap()
    bstr = nc.alloc_sbuf_tensor("bstr", [128, 64], F32).ap()
    wln1s = nc.alloc_sbuf_tensor("wln1s", [128, 16], F32).ap()
    sublnb = nc.alloc_sbuf_tensor("sublnb", [128, 256], F32).ap()
    comb = nc.alloc_sbuf_tensor("comb", [128, 64], F32).ap()
    r1rowb = nc.alloc_sbuf_tensor("r1rowb", [1, 2048], BF16).ap()
    r1ownb = nc.alloc_sbuf_tensor("r1ownb", [1, 1024], BF16).ap()

    PS = [nc.alloc_psum_tensor(f"ps{i}", [128, 512], F32).ap() for i in range(6)]
    PT = nc.alloc_psum_tensor("pt", [128, 256], BF16).ap()
    PT2 = nc.alloc_psum_tensor("pt2", [128, 256], BF16).ap()
    PS5 = PS[5]

    assert nc.sbuf_bytes_remaining >= 0, nc.sbuf_bytes_remaining

    xnT_all = [R1.v(ht * 4096, 2048, BF16) for ht in range(16)]
    xnT_own = [R2.v(ht * 2048, 1024, BF16) for ht in range(16)]
    actT = [R3.v(et * 2048, 1024, BF16) for et in range(16)]
    Pb = [R4.v(kt * 1024, 512, BF16) for kt in range(16)]
    x1v = [R1.v(tt * 8192, 2048, F32) for tt in range(8)]
    h2T = [R2.v(ht * 2048, 1024, BF16) for ht in range(16)]
    guv = [R3.v(it * 2048, 1024, BF16) for it in range(11)]

    env.update(locals())
    p = Plan()
    bank_free = {}

    def bank_wait(eng, key):
        stx = bank_free.get(key)
        if stx is not None:
            p.w(eng, stx[0], stx[1])

    def bank_rel(key, sem):
        bank_free[key] = (sem, p.c(sem))

    def bank_rel_t(key, tok):
        bank_free[key] = tok

    LANES = {"xl": 3, "wl": 4, "st": 2, "sg": 2}
    lane_rr = {k: 0 for k in LANES}

    def dma_lane(qeng, prefix, fn):
        k = lane_rr[prefix]
        lane_rr[prefix] = (k + 1) % LANES[prefix]
        sem = f"{prefix}{k}"
        p.w(qeng, sem, p.c(sem))
        p.o(qeng, fn, inc=sem, k=16)
        return (sem, p.c(sem))

    def w_all(eng, prefix):
        for k in range(LANES[prefix]):
            p.w(eng, f"{prefix}{k}", None)

    def dma_y(dst, src):
        return dma_lane("y", "xl", lambda e, d=dst, s=src: e.dma_start(out=d, in_=s))

    def dma_g(dst, src):
        return dma_lane("g", "wl", lambda e, d=dst, s=src: e.dma_start(out=d, in_=s))

    def store_y(dst, src):
        return dma_lane("y", "st", lambda e, d=dst, s=src: e.dma_start(out=d, in_=s))

    def store_g(dst, src):
        return dma_lane("g", "sg", lambda e, d=dst, s=src: e.dma_start(out=d, in_=s))

    def v_op(fn, inc=None):
        return p.o("v", fn, inc=inc)

    def s_op(fn, inc=None):
        return p.o("s", fn, inc=inc)

    def t_op(fn, inc=None):
        return p.o("t", fn, inc=inc)

    def rsqrt_cols(src, dst, tmp):
        p.w("s", "ve", None)
        s_op(lambda e: e.activation(out=tmp, in_=src, func=AF.Sqrt), inc="sc")
        p.w("v", "sc", None)
        v_op(lambda e: e.reciprocal(dst, tmp))
        p.drain("v")
        v_op(lambda e: e.tensor_tensor(tmp, src, dst, ALU.mult))
        p.drain("v")
        v_op(lambda e: e.tensor_tensor(tmp, tmp, dst, ALU.mult))
        p.drain("v")
        v_op(lambda e: e.tensor_scalar(out=tmp, in0=tmp, scalar1=-0.5, scalar2=1.5,
                                       op0=ALU.mult, op1=ALU.add))
        p.drain("v")
        v_op(lambda e: e.tensor_tensor(dst, dst, tmp, ALU.mult), inc="ve")
        p.drain("v")

    def bcast_hilo(row_f32, n, hf, hb, lb, out_f32):
        p.w("s", "ve", None)
        s_op(lambda e: e.copy(out=hb[0:1, 0:n], in_=row_f32[0:1, 0:n]))
        p.drain("s")
        s_op(lambda e: e.copy(out=hf[0:1, 0:n], in_=hb[0:1, 0:n]), inc="sc")
        p.w("v", "sc", None)
        v_op(lambda e: e.tensor_tensor(hf[0:1, 0:n], row_f32[0:1, 0:n], hf[0:1, 0:n],
                                       ALU.subtract), inc="ve")
        p.w("s", "ve", None)
        s_op(lambda e: e.copy(out=lb[0:1, 0:n], in_=hf[0:1, 0:n]), inc="sc")
        for c0 in range(0, n, 512):
            cw = min(512, n - c0)
            bank_wait("t", 5)
            p.w("t", "sc", None)
            t_op(lambda e, c0=c0, cw=cw: e.matmul(PS5[:, 0:cw], ones_b[0:1, 0:128],
                                                  hb[0:1, c0:c0 + cw], start=True, stop=False))
            t_op(lambda e, c0=c0, cw=cw: e.matmul(PS5[:, 0:cw], ones_b[0:1, 0:128],
                                                  lb[0:1, c0:c0 + cw], start=False, stop=True),
                 inc="mm")
            p.w("s", "mm", None)
            s_op(lambda e, c0=c0, cw=cw: e.copy(out=out_f32[:, c0:c0 + cw], in_=PS5[:, 0:cw]),
                 inc="sc")
            bank_rel(5, "sc")

    def transpose_pair(srcA, srcB, dstA, dstB, tb_key, tb):
        bank_wait("t", tb_key)
        t_op(lambda e: e.transpose(tb[:, 0:128], srcA, identb))
        t_op(lambda e: e.transpose(tb[:, 128:256], srcB, identb), inc="mm")
        p.w("s", "mm", None)
        s_op(lambda e: e.copy(out=dstA, in_=tb[:, 0:128]))
        s_op(lambda e: e.copy(out=dstB, in_=tb[:, 128:256]), inc="sc")
        bank_rel(tb_key, "sc")

    def transpose_one(src, dst, tb_key, tb):
        bank_wait("t", tb_key)
        t_op(lambda e: e.transpose(tb[:, 0:128], src, identb), inc="mm")
        p.w("s", "mm", None)
        s_op(lambda e: e.copy(out=dst, in_=tb[:, 0:128]), inc="sc")
        bank_rel(tb_key, "sc")

    SC_TMP, SC_SSQ, SC_A, SC_RSTD, SC_T1, SC_R1ARG, SC_R1F, SC_INV, SC_Z = range(9)

    def st_col(c, w=1):
        return stat[:, c:c + w]

    def emit_taps(stage):
        for stg, name, fn, shape in taps:
            if stg != stage:
                continue
            for sem in ("mm", "ve", "sc"):
                p.w("g", sem, None)
            w_all("g", "wl")
            w_all("g", "xl")
            w_all("g", "st")
            w_all("g", "sg")
            off = 0
            for src_ap, rows in fn():
                store_g(tap_out[name][off:off + rows, :], src_ap)
                off += rows

    def phase_guard(engs=("g", "y")):
        for e in engs:
            for sem in ("mm", "ve", "sc"):
                p.w(e, sem, None)
            w_all(e, "st")
            w_all(e, "sg")

    LP_hf, LP_hb, LP_lb, LP_b32, LP_lam, LP_srow = 0, 8192, 12288, 16384, 18432, 20480
    hf_row = LAR.v(LP_hf, 2048, F32)
    hb_row = LAR.v(LP_hb, 2048, BF16)
    lb_row = LAR.v(LP_lb, 2048, BF16)
    b32 = LAR.v(LP_b32, 512, F32)
    lamrow = LAR.v(LP_lam, 512, F32)
    srow = LAR.v(LP_srow, 256, F32)

    dma_y(identf, ident_in)
    dma_y(masks, masks_in)
    dma_y(lamrow[0:1, :], lamv[None, :])
    dma_y(srow[0:1, :], sublnw[None, :])
    dma_y(b32[0:32, 0:128], wqkv_b.rearrange("(c p) -> c p", p=128))
    dma_y(b32[0:16, 128:256], bqkv3[0, :].rearrange("(c p) -> c p", p=128))
    dma_y(b32[0:16, 256:384], bqkv3[1, :].rearrange("(c p) -> c p", p=128))
    dma_y(b32[0:16, 384:512], w_ln1.rearrange("(c p) -> c p", p=128))
    dma_g(identb, ident_in)
    dma_g(spermb, sperm_in)
    v_op(lambda e: e.memset(ones_b[:, :], 1.0))
    v_op(lambda e: e.memset(negC[:, :], CSHIFT), inc="ve")

    w_all("t", "xl")
    w_all("t", "wl")
    p.w("t", "ve", None)
    for (c0, w, dst) in ((0, 32, bstr[:, 0:32]), (128, 16, bstr[:, 32:48]),
                         (256, 16, bstr[:, 48:64]), (384, 16, wln1s[:, 0:16])):
        bank_wait("t", 4)
        t_op(lambda e, c0=c0, w=w: e.transpose(PS[4][0:128, 0:w], b32[0:w, c0:c0 + 128],
                                               identf[0:w, 0:w]), inc="mm")
        p.w("s", "mm", None)
        s_op(lambda e, w=w, dst=dst: e.copy(out=dst, in_=PS[4][:, 0:w]), inc="sc")
        bank_rel(4, "sc")

    w_all("v", "xl")
    v_op(lambda e: e.tensor_tensor(hf_row[0:1, 0:128], lamrow[0:1, 0:128],
                                   lamrow[0:1, 128:256], ALU.mult))
    p.drain("v")
    v_op(lambda e: e.reduce_sum(stat[0:1, 0:1], hf_row[0:1, 0:128], axis=AX.X))
    p.drain("v")
    v_op(lambda e: e.tensor_tensor(hf_row[0:1, 0:128], lamrow[0:1, 256:384],
                                   lamrow[0:1, 384:512], ALU.mult))
    p.drain("v")
    v_op(lambda e: e.reduce_sum(stat[0:1, 1:2], hf_row[0:1, 0:128], axis=AX.X), inc="ve")
    p.w("s", "ve", None)
    s_op(lambda e: e.activation(out=stat[0:1, 2:3], in_=stat[0:1, 0:1], func=AF.Exp))
    p.drain("s")
    s_op(lambda e: e.activation(out=stat[0:1, 3:4], in_=stat[0:1, 1:2], func=AF.Exp), inc="sc")
    p.w("v", "sc", None)
    v_op(lambda e: e.tensor_tensor(stat[0:1, 4:5], stat[0:1, 3:4], stat[0:1, 2:3],
                                   ALU.subtract))
    p.drain("v")
    v_op(lambda e: e.tensor_scalar(out=stat[0:1, 4:5], in0=stat[0:1, 4:5],
                                   scalar1=-LAMBDA_INIT, scalar2=None, op0=ALU.add), inc="ve")
    p.drain("v")
    bcast_hilo(stat[0:1, 4:5], 1, hf_row, hb_row, lb_row, lamsc)
    bcast_hilo(srow, 256, hf_row, hb_row, lb_row, sublnb)
    emit_taps("prologue")

    if last_idx >= 1:
        phase_guard()
        xst = [LAR.v(0, 2048, F32), LAR.v(8192, 2048, F32)]
        xnbs = [LAR.v(16384, 2048, BF16), LAR.v(20480, 2048, BF16)]
        trash = LAR.v(24576, 2048, F32)

        for (src, ntiles, dstT, rowdst) in ((x_all, 16, xnT_all, r1rowb),
                                            (x_ownd, 8, xnT_own, r1ownb)):
            for t in range(ntiles):
                buf = xst[t % 2]
                xnb = xnbs[t % 2]
                sb = (t % 2) * 8
                bank_wait("y", f"xst{t % 2}")
                dma_y(buf, src[t * 128:(t + 1) * 128, :])
                bank_wait("v", f"stat{t % 2}")
                w_all("v", "xl")
                v_op(lambda e, c=sb, b=buf: e.reduce_sum(st_col(c + SC_TMP), b, axis=AX.X))
                p.drain("v")
                v_op(lambda e, c=sb: e.tensor_scalar(out=st_col(c + SC_TMP),
                                                     in0=st_col(c + SC_TMP),
                                                     scalar1=-1.0 / 2048, scalar2=None,
                                                     op0=ALU.mult))
                p.drain("v")
                v_op(lambda e, b=buf, c=sb: e.tensor_scalar(out=b, in0=b,
                                                            scalar1=st_col(c + SC_TMP),
                                                            scalar2=None, op0=ALU.add),
                     inc="ve")
                p.w("s", "ve", None)
                s_op(lambda e, b=buf, c=sb: e.activation(out=trash, in_=b, func=AF.Square,
                                                         accum_out=st_col(c + 1)), inc="sc")
                p.w("v", "sc", None)
                v_op(lambda e, c=sb: e.tensor_scalar(out=st_col(c + 2), in0=st_col(c + 1),
                                                     scalar1=1.0 / 2048, scalar2=1e-5,
                                                     op0=ALU.mult, op1=ALU.add))
                p.drain("v")
                v_op(lambda e, c=sb: e.reciprocal(st_col(c + 7), st_col(c + 2)))
                p.drain("v")
                v_op(lambda e, c=sb: e.tensor_scalar(out=st_col(c + 3), in0=st_col(c + 1),
                                                     scalar1=1.0 / 2048, scalar2=st_col(c + 7),
                                                     op0=ALU.mult, op1=ALU.mult))
                p.drain("v")
                v_op(lambda e, c=sb: e.tensor_scalar(out=st_col(c + 3), in0=st_col(c + 3),
                                                     scalar1=1e-6, scalar2=None, op0=ALU.add),
                     inc="ve")
                p.drain("v")
                rsqrt_cols(stat[:, sb + 2:sb + 4], stat[:, sb + 4:sb + 6],
                           stat[:, sb + 6:sb + 8])
                v_op(lambda e, b=buf, xnb=xnb, c=sb:
                     e.tensor_scalar(out=xnb, in0=b, scalar1=st_col(c + 4),
                                     scalar2=None, op0=ALU.mult), inc="ve")
                bank_rel(f"xst{t % 2}", "ve")
                p.w("t", "ve", None)
                for hp in range(8):
                    tb_key, tb = ("PT", PT) if hp % 2 == 0 else ("PT2", PT2)
                    transpose_pair(xnb[:, (2 * hp) * 128:(2 * hp + 1) * 128],
                                   xnb[:, (2 * hp + 1) * 128:(2 * hp + 2) * 128],
                                   dstT[2 * hp][:, t * 128:(t + 1) * 128],
                                   dstT[2 * hp + 1][:, t * 128:(t + 1) * 128],
                                   tb_key, tb)
                bank_wait("t", 4)
                t_op(lambda e, c=sb: e.transpose(PS[4][0:1, 0:128], st_col(c + 5), identf),
                     inc="mm")
                p.w("s", "mm", None)
                s_op(lambda e, rowdst=rowdst, t=t:
                     e.copy(out=rowdst[0:1, t * 128:(t + 1) * 128], in_=PS[4][0:1, 0:128]),
                     inc="sc")
                bank_rel(4, "sc")
                bank_rel(f"stat{t % 2}", "mm")
        emit_taps("stage0")

    def chunk_smin(ch, kt):
        for sl in range(4):
            if slot_extent(4 * ch + sl) > kt:
                return sl
        return 4

    def attn_unit(KTsrc, kbase, Qsrc, qbase, Vsrc, vstride, vw, epilogue, sc_banks=(0, 1, 4, 5),
                  cx_banks=(2, 3)):
        p.w("t", "sc", None)
        p.w("t", "ve", None)
        for ch in range(2):
            kmax = slot_extent(4 * ch + 3)
            for kt in range(kmax):
                smin = chunk_smin(ch, kt)
                qoff = smin * 128
                psk = sc_banks[kt % len(sc_banks)]
                bank_wait("t", psk)
                t_op(lambda e, kt=kt, ch=ch, qoff=qoff, psk=psk:
                     e.matmul(PS[psk][:, qoff:512],
                              KTsrc[:, kbase + kt * 128: kbase + (kt + 1) * 128],
                              Qsrc[:, qbase + ch * 512 + qoff: qbase + (ch + 1) * 512],
                              start=True, stop=True), inc="mm")
                mlist = []
                for sl in range(4):
                    gs = 4 * ch + sl
                    if kt == slot_extent(gs) - 2:
                        mlist.append((sl, 0))
                    elif kt == slot_extent(gs) - 1:
                        mlist.append((sl, 1))
                if mlist:
                    p.w("v", "mm", None)
                    for sl, m in mlist:
                        v_op(lambda e, sl=sl, m=m, psk=psk:
                             e.tensor_tensor(PS[psk][:, sl * 128:(sl + 1) * 128],
                                             PS[psk][:, sl * 128:(sl + 1) * 128],
                                             masks[:, m * 128:(m + 1) * 128], ALU.add),
                             inc="ve")
                    p.w("s", "ve", None)
                p.w("s", "mm", None)
                s_op(lambda e, kt=kt, qoff=qoff, psk=psk:
                     e.activation(out=Pb[kt][:, qoff:512], in_=PS[psk][:, qoff:512],
                                  func=AF.Exp, bias=negC[:, 0:1], scale=DSCALE), inc="sc")
                bank_rel(psk, "sc")
            p.w("t", "sc", None)
            for sl in range(4):
                gs = 4 * ch + sl
                E = slot_extent(gs)
                pck = cx_banks[sl % len(cx_banks)]
                bank_wait("t", pck)
                for kt in range(E):
                    t_op(lambda e, kt=kt, sl=sl, pck=pck, E=E:
                         e.matmul(PS[pck][:, 0:vw],
                                  Pb[kt][:, sl * 128:(sl + 1) * 128],
                                  Vsrc[:, kt * vstride: kt * vstride + vw],
                                  start=(kt == 0), stop=(kt == E - 1)),
                         inc="mm" if kt == E - 1 else None)
                epilogue(gs, pck)
        return

    if last_idx >= 2:
        phase_guard()
        VW = 257
        wregA = LAR.v(0, 2048, BF16)
        wregB = LAR.v(4096, 2048, BF16)
        wregV = LAR.v(8192, 4096, BF16)
        QTb = LAR.v(16384, 2048, BF16)
        KTb = LAR.v(20480, 4096, BF16)
        Vx = LAR.v(28672, 16 * VW, BF16)
        opair = LAR.v(37120, 2048, F32)
        vbias_row = LAR.v(45312, 1024, BF16)
        sevac = LAR.v(47360, 512, F32)
        obf = LAR.v(49408, 256, BF16)
        obrow = LAR.v(50176, 256, BF16)
        wstrip = [LAR.v(0, 4096, BF16), LAR.v(8192, 4096, BF16)]
        build_nc.env2 = {"QTb": QTb, "KTb": KTb, "Vx": Vx, "opair": opair, "wregV": wregV, "vbias_row": vbias_row, "lamsc": lamsc, "stat": stat, "s_spill": s_spill}

        vbtok = dma_g(vbias_row[0:1, :], wqkv_b[None, 3072:4096])
        for tt in (range(16) if DIFF_MEMSET else []):
            v_op(lambda e, tt=tt: e.memset(Vx[:, tt * VW + 256: tt * VW + 257], 1.0),
                 inc="ve" if tt == 15 else None)
        if not DIFF_MEMSET:
            v_op(lambda e: e.engine_nop(), inc="ve")

        def proj_ct(wreg, wtok, srcT, tcols, dst, dstbase, bias_col):
            for ch in range(tcols // 512):
                psk = ch % 2
                bank_wait("t", psk)
                p.w("t", *wtok)
                for ht in range(16):
                    t_op(lambda e, ht=ht, ch=ch, wreg=wreg, psk=psk:
                         e.matmul(PS[psk][:, 0:512], wreg[:, ht * 128:(ht + 1) * 128],
                                  srcT[ht][:, ch * 512:(ch + 1) * 512],
                                  start=(ht == 0), stop=(ht == 15)),
                         inc="mm" if ht == 15 else None)
                p.w("s", "mm", None)
                s_op(lambda e, ch=ch, psk=psk:
                     e.activation(out=dst[:, dstbase + ch * 512: dstbase + (ch + 1) * 512],
                                  in_=PS[psk][:, 0:512], func=AF.Identity,
                                  bias=bstr[:, bias_col:bias_col + 1], scale=1.0), inc="sc")
                bank_rel(psk, "sc")

        for sp in range(DIFF_NSP):
            bank_wait("g", "wregV")
            vtok = dma_g(wregV, wqkv_v[sp])
            p.w("t", "ve", None)
            for tt in (range(DIFF_NVT) if DIFF_PHASE >= 1 else ()):
                psk = tt % 2
                bank_wait("t", psk)
                p.w("t", *vtok)
                p.w("t", *vbtok)
                for ht in range(16):
                    t_op(lambda e, ht=ht, tt=tt, psk=psk:
                         e.matmul(PS[psk][:, 0:256], xnT_all[ht][:, tt * 128:(tt + 1) * 128],
                                  wregV[:, ht * 256:(ht + 1) * 256],
                                  start=(ht == 0), stop=False))
                t_op(lambda e, sp=sp, psk=psk:
                     e.matmul(PS[psk][:, 0:256], ones_b[0:1, 0:128],
                              vbias_row[0:1, sp * 256:(sp + 1) * 256],
                              start=False, stop=True), inc="mm")
                p.w("s", "mm", None)
                s_op(lambda e, tt=tt, psk=psk: e.copy(out=Vx[:, tt * VW:tt * VW + 256],
                                                      in_=PS[psk][:, 0:256]), inc="sc")
                bank_rel(psk, "sc")
            bank_rel("wregV", "mm")
            for comp in (range(2) if DIFF_PHASE >= 2 else ()):
                wreg, key = (wregA, "wregA") if comp == 0 else (wregB, "wregB")
                bank_wait("g", key)
                ct = 16 + sp * 2 + comp
                ktok = dma_g(wreg, wqkv_qk[ct])
                proj_ct(wreg, ktok, xnT_all, 2048, KTb, comp * 2048, ct)
                bank_rel(key, "mm")
            for hq in ((2 * sp, 2 * sp + 1) if DIFF_PHASE >= 3 else ()):
                for comp in range(2):
                    wreg, key = (wregA, "wregA") if comp == 0 else (wregB, "wregB")
                    bank_wait("g", key)
                    ct = hq * 2 + comp
                    qtok = dma_g(wreg, wqkv_qk[ct])
                    proj_ct(wreg, qtok, xnT_own, 1024, QTb, comp * 1024, ct)
                    bank_rel(key, "mm")

                def diff_epi(gs, pck, comp):
                    p.w("v", "mm", None)
                    p.drain("v")
                    v_op(lambda e, pck=pck: e.reciprocal(st_col(SC_Z), PS[pck][:, 256:257]))
                    p.drain("v")
                    if comp == 1:
                        v_op(lambda e: e.tensor_tensor(st_col(SC_Z), st_col(SC_Z),
                                                       lamsc[:, 0:1], ALU.mult))
                        p.drain("v")
                        v_op(lambda e, gs=gs, pck=pck:
                             e.tensor_scalar(out=opair[:, gs * 256:(gs + 1) * 256],
                                             in0=PS[pck][:, 0:256], scalar1=st_col(SC_Z),
                                             scalar2=None, op0=ALU.mult), inc="ve")
                    else:
                        v_op(lambda e, gs=gs, pck=pck:
                             e.scalar_tensor_tensor(out=opair[:, gs * 256:(gs + 1) * 256],
                                                    in0=PS[pck][:, 0:256],
                                                    scalar=st_col(SC_Z),
                                                    in1=opair[:, gs * 256:(gs + 1) * 256],
                                                    op0=ALU.mult, op1=ALU.add), inc="ve")
                    bank_rel(pck, "ve")

                if not DIFF_ATTN:
                    continue
                for comp in (1, 0):
                    attn_unit(KTb, comp * 2048, QTb, comp * 1024, Vx, VW, VW,
                              lambda gs, pck, comp=comp: diff_epi(gs, pck, comp))
                bank_rel("VxRead", "mm")
                p.w("s", "ve", None)
                for sl in range(8):
                    s_op(lambda e, sl=sl: e.activation(out=sevac[:, 0:256],
                                                       in_=opair[:, sl * 256:(sl + 1) * 256],
                                                       func=AF.Square,
                                                       accum_out=stat[:, 16 + sl:17 + sl]),
                         inc="sc")
                    p.drain("s")
                p.w("v", "sc", None)
                v_op(lambda e: e.tensor_scalar(out=stat[:, 24:32], in0=stat[:, 16:24],
                                               scalar1=1.0 / 256, scalar2=1e-5,
                                               op0=ALU.mult, op1=ALU.add), inc="ve")
                p.drain("v")
                rsqrt_cols(stat[:, 24:32], stat[:, 32:40], stat[:, 40:48])
                v_op(lambda e: e.tensor_scalar(out=stat[:, 32:40], in0=stat[:, 32:40],
                                               scalar1=(1.0 - LAMBDA_INIT), scalar2=None,
                                               op0=ALU.mult), inc="ve")
                p.drain("v")
                for sl in range(8):
                    p.w("v", "mm", None)
                    v_op(lambda e, sl=sl:
                         e.scalar_tensor_tensor(out=obf[:, 0:256],
                                                in0=opair[:, sl * 256:(sl + 1) * 256],
                                                scalar=stat[:, 32 + sl:33 + sl],
                                                in1=sublnb[:, 0:256],
                                                op0=ALU.mult, op1=ALU.mult), inc="ve")
                    p.w("t", "ve", None)
                    transpose_pair(obf[:, 0:128], obf[:, 128:256],
                                   actT[2 * hq][:, sl * 128:(sl + 1) * 128],
                                   actT[2 * hq + 1][:, sl * 128:(sl + 1) * 128],
                                   "PT" if sl % 2 == 0 else "PT2",
                                   PT if sl % 2 == 0 else PT2)

        phase_guard(("g",))
        p.w("t", "sc", None)
        if not DIFF_OUTPROJ:
            emit_taps("diff")
        for js in range(8):
            wkey = f"wstrip{js % 2}"
            bank_wait("g", wkey)
            wtok1 = dma_g(wstrip[js % 2], outwT_t[js])
            bank_wait("g", "obrow")
            wtok2 = dma_g(obrow[0:1, 0:256], outb[None, js * 256:(js + 1) * 256])
            for tt in range(8):
                psk = tt % 2
                bank_wait("t", psk)
                p.w("t", *wtok1)
                p.w("t", *wtok2)
                for et in range(16):
                    t_op(lambda e, et=et, tt=tt, js=js, psk=psk:
                         e.matmul(PS[psk][:, 0:256], actT[et][:, tt * 128:(tt + 1) * 128],
                                  wstrip[js % 2][:, et * 256:(et + 1) * 256],
                                  start=(et == 0), stop=False))
                t_op(lambda e, psk=psk:
                     e.matmul(PS[psk][:, 0:256], ones_b[0:1, 0:128],
                              obrow[0:1, 0:256],
                              start=False, stop=True), inc="mm")
                par = tt % 2
                bank_wait("s", f"sevac{par}")
                p.w("s", "mm", None)
                s_op(lambda e, par=par, psk=psk:
                     e.copy(out=sevac[:, par * 256:(par + 1) * 256], in_=PS[psk][:, 0:256]),
                     inc="sc")
                bank_rel(psk, "sc")
                p.w("y", "sc", None)
                stok = store_y(s_spill[tt * 128:(tt + 1) * 128, js * 256:(js + 1) * 256],
                               sevac[:, par * 256:(par + 1) * 256])
                bank_rel_t(f"sevac{par}", stok)
            bank_rel(wkey, "mm")
            bank_rel("obrow", "mm")
        if DIFF_OUTPROJ:
            emit_taps("diff")

    if last_idx >= 3:
        phase_guard()
        wsl = [LAR.v(i * 4096, 2048, BF16) for i in range(6)]
        cosq = LAR.v(24576, 1024, BF16)
        sinq = LAR.v(26624, 1024, BF16)
        cosk = LAR.v(28672, 2048, BF16)
        sink = LAR.v(32768, 2048, BF16)
        QTr = LAR.v(36864, 1024, BF16)
        KTr = LAR.v(38912, 2048, BF16)
        VWF = 132
        Vx2 = LAR.v(43008, 16 * VWF, BF16)
        bvrow = LAR.v(47232, 128, BF16)
        ctxbf = LAR.v(47520, 128, BF16) if False else LAR.v(47488, 128, BF16)
        rtmp = LAR.v(47744, 1024, BF16)
        tabA = LAR.v(0, 2048, F32)
        tabB = LAR.v(8192, 2048, F32)
        build_nc.env3 = {"QTr": QTr, "KTr": KTr, "Vx2": Vx2, "x1v": x1v, "cosq": cosq,
                         "sinq": sinq, "cosk": cosk, "sink": sink}
        wostrip = [LAR.v(0, 8192, BF16), LAR.v(16384, 8192, BF16)]
        xtile = LAR.v(32768, 2048, F32)
        trash2 = LAR.v(40960, 2048, F32)

        for (row, tgt, nblk) in ((r1rowb, xnT_all, 4), (r1ownb, xnT_own, 2)):
            for tb in range(nblk):
                bank_wait("t", 5)
                p.w("t", "sc", None)
                t_op(lambda e, row=row, tb=tb:
                     e.matmul(PS5[:, 0:512], ones_b[0:1, 0:128],
                              row[0:1, tb * 512:(tb + 1) * 512], start=True, stop=True),
                     inc="mm")
                p.w("v", "mm", None)
                for ht in range(16):
                    v_op(lambda e, ht=ht, tb=tb, tgt=tgt:
                         e.scalar_tensor_tensor(out=tgt[ht][:, tb * 512:(tb + 1) * 512],
                                                in0=tgt[ht][:, tb * 512:(tb + 1) * 512],
                                                scalar=wln1s[:, ht:ht + 1],
                                                in1=PS5[:, 0:512],
                                                op0=ALU.mult, op1=ALU.mult),
                         inc="ve" if ht == 15 else None)
                bank_rel(5, "ve")

        for (dst, n, hi, lo) in ((cosq, 1024, ctabq[0], ctabq[1]),
                                 (sinq, 1024, ctabq[2], ctabq[3]),
                                 (cosk, 2048, ctabk[0], ctabk[1]),
                                 (sink, 2048, ctabk[2], ctabk[3])):
            bank_wait("y", "tabAB")
            dma_y(tabA[:, 0:n], hi)
            dma_y(tabB[:, 0:n], lo)
            w_all("v", "xl")
            v_op(lambda e, dst=dst, n=n: e.tensor_tensor(dst[:, 0:n], tabA[:, 0:n],
                                                         tabB[:, 0:n], ALU.add), inc="ve")
            bank_rel("tabAB", "ve")
        phase_guard(("g",))

        def rope(raw, costab, sintab, nchunk, rot_banks=(2, 3)):
            p.w("t", "sc", None)
            p.w("t", "ve", None)
            for ch in range(nchunk):
                rb = rot_banks[ch % 2]
                bank_wait("t", rb)
                t_op(lambda e, ch=ch, raw=raw, rb=rb:
                     e.matmul(PS[rb][:, 0:512], spermb,
                              raw[:, ch * 512:(ch + 1) * 512], start=True, stop=True),
                     inc="mm")
                p.w("v", "mm", None)
                v_op(lambda e, ch=ch, raw=raw, costab=costab:
                     e.tensor_tensor(raw[:, ch * 512:(ch + 1) * 512],
                                     raw[:, ch * 512:(ch + 1) * 512],
                                     costab[:, ch * 512:(ch + 1) * 512], ALU.mult))
                v_op(lambda e, ch=ch, sintab=sintab, rb=rb:
                     e.tensor_tensor(rtmp[:, (ch % 2) * 512:(ch % 2 + 1) * 512],
                                     PS[rb][:, 0:512],
                                     sintab[:, ch * 512:(ch + 1) * 512], ALU.mult))
                p.drain("v")
                v_op(lambda e, ch=ch, raw=raw:
                     e.tensor_tensor(raw[:, ch * 512:(ch + 1) * 512],
                                     raw[:, ch * 512:(ch + 1) * 512],
                                     rtmp[:, (ch % 2) * 512:(ch % 2 + 1) * 512], ALU.add),
                     inc="ve")
                bank_rel(rb, "ve")

        for h in range(16):
            gsl = 3 * (h % 2)
            wlv = []
            for (mat, idx) in ((wq_t, 0), (wk_t, 1), (wv_t, 2)):
                bank_wait("g", f"wsl{gsl + idx}")
                wlv.append(dma_g(wsl[gsl + idx], mat[h]))
            bank_wait("g", "bvrow")
            bvtok = dma_g(bvrow[0:1, 0:128], bqkv3[2, h * 128:(h + 1) * 128][None, :])
            p.w("s", "mm", None)
            p.w("v", "mm", None)
            proj_ct(wsl[gsl + 0], wlv[0], xnT_own, 1024, QTr, 0, 32 + h)
            proj_ct(wsl[gsl + 1], wlv[1], xnT_all, 2048, KTr, 0, 48 + h)
            rope(QTr, cosq, sinq, 2)
            rope(KTr, cosk, sink, 4)
            p.w("v", "mm", None)
            v_op(lambda e: e.memset(Vx2[:, :], 1.0), inc="ve")
            for tt in range(16):
                psk = tt % 2
                bank_wait("t", psk)
                p.w("t", *wlv[2])
                p.w("t", *bvtok)
                for ht in range(16):
                    t_op(lambda e, ht=ht, tt=tt, gsl=gsl, psk=psk:
                         e.matmul(PS[psk][:, 0:128], xnT_all[ht][:, tt * 128:(tt + 1) * 128],
                                  wsl[gsl + 2][:, ht * 128:(ht + 1) * 128],
                                  start=(ht == 0), stop=False))
                t_op(lambda e, psk=psk:
                     e.matmul(PS[psk][:, 0:128], ones_b[0:1, 0:128], bvrow[0:1, 0:128],
                              start=False, stop=True), inc="mm")
                p.w("s", "ve", None)
                p.w("s", "mm", None)
                s_op(lambda e, tt=tt, psk=psk:
                     e.copy(out=Vx2[:, tt * VWF:tt * VWF + 128], in_=PS[psk][:, 0:128]),
                     inc="sc")
                bank_rel(psk, "sc")
            bank_rel(f"wsl{gsl}", "mm")
            bank_rel(f"wsl{gsl + 1}", "mm")
            bank_rel(f"wsl{gsl + 2}", "mm")
            bank_rel("bvrow", "mm")

            def ffn_epi(gs, pck, h=h):
                p.w("v", "mm", None)
                p.drain("v")
                v_op(lambda e, pck=pck: e.reciprocal(st_col(SC_Z), PS[pck][:, 128:129]))
                p.drain("v")
                v_op(lambda e, pck=pck: e.tensor_scalar(out=ctxbf[:, 0:128],
                                                        in0=PS[pck][:, 0:128],
                                                        scalar1=st_col(SC_Z), scalar2=None,
                                                        op0=ALU.mult), inc="ve")
                bank_rel(pck, "ve")
                p.w("t", "ve", None)
                transpose_one(ctxbf[:, 0:128], actT[h][:, gs * 128:(gs + 1) * 128],
                              "PT" if gs % 2 == 0 else "PT2", PT if gs % 2 == 0 else PT2)

            attn_unit(KTr, 0, QTr, 0, Vx2, VWF, 129, ffn_epi)

        phase_guard(("g", "y"))
        for js in range(4):
            wkey = f"wostrip{js % 2}"
            bank_wait("g", wkey)
            wotok = dma_g(wostrip[js % 2], wo_t[js])
            for tt in range(8):
                psk = tt % 2
                bank_wait("t", psk)
                p.w("t", *wotok)
                for et in range(16):
                    t_op(lambda e, et=et, tt=tt, js=js, psk=psk:
                         e.matmul(PS[psk][:, 0:512], actT[et][:, tt * 128:(tt + 1) * 128],
                                  wostrip[js % 2][:, et * 512:(et + 1) * 512],
                                  start=(et == 0), stop=(et == 15)), inc="mm")
                p.w("v", "mm", None)
                v_op(lambda e, tt=tt, js=js, psk=psk:
                     e.tensor_copy(out=x1v[tt][:, js * 512:(js + 1) * 512],
                                   in_=PS[psk][:, 0:512]), inc="ve")
                bank_rel(psk, "ve")
            bank_rel(wkey, "mm")
        for tt in range(8):
            p.w("y", "ve", None)
            dma_y(xtile, x_ownd[tt * 128:(tt + 1) * 128, :])
            w_all("v", "xl")
            v_op(lambda e: e.reduce_sum(st_col(SC_TMP), xtile, axis=AX.X))
            p.drain("v")
            v_op(lambda e: e.tensor_scalar(out=st_col(SC_TMP), in0=st_col(SC_TMP),
                                           scalar1=-1.0 / 2048, scalar2=None, op0=ALU.mult))
            p.drain("v")
            v_op(lambda e: e.tensor_scalar(out=xtile, in0=xtile, scalar1=st_col(SC_TMP),
                                           scalar2=None, op0=ALU.add), inc="ve")
            p.w("s", "ve", None)
            s_op(lambda e: e.activation(out=trash2, in_=xtile, func=AF.Square,
                                        accum_out=st_col(SC_SSQ)), inc="sc")
            p.w("v", "sc", None)
            v_op(lambda e: e.tensor_scalar(out=st_col(SC_A), in0=st_col(SC_SSQ),
                                           scalar1=1.0 / 2048, scalar2=1e-5,
                                           op0=ALU.mult, op1=ALU.add), inc="ve")
            p.drain("v")
            rsqrt_cols(st_col(SC_A), st_col(SC_RSTD), st_col(SC_T1))
            v_op(lambda e: e.tensor_scalar(out=xtile, in0=xtile, scalar1=st_col(SC_RSTD),
                                           scalar2=None, op0=ALU.mult))
            p.drain("v")
            v_op(lambda e, tt=tt: e.tensor_tensor(x1v[tt], x1v[tt], xtile, ALU.add),
                 inc="ve")
        emit_taps("ffn")

    if last_idx >= 4:
        phase_guard()
        wln2row = LAR.v(0, 2048, F32)
        hfm = LAR.v(8192, 2048, F32)
        hbm = LAR.v(16384, 2048, BF16)
        lbm = LAR.v(20480, 2048, BF16)
        wln2b = LAR.v(24576, 2048, F32)
        h2stg = LAR.v(32768, 2048, BF16)
        trash3 = LAR.v(36864, 2048, F32)
        wrouterb = LAR.v(45056, 128, BF16)

        dma_y(wln2row[0:1, :], w_ln2[None, :])
        w_all("v", "xl")
        v_op(lambda e: e.engine_nop(), inc="ve")
        bcast_hilo(wln2row, 2048, hfm, hbm, lbm, wln2b)
        p.w("v", "sc", None)
        for tt in range(8):
            p.w("s", "ve", None)
            s_op(lambda e, tt=tt: e.activation(out=trash3, in_=x1v[tt], func=AF.Square,
                                               accum_out=st_col(SC_SSQ)), inc="sc")
            p.w("v", "sc", None)
            v_op(lambda e: e.tensor_scalar(out=st_col(SC_A), in0=st_col(SC_SSQ),
                                           scalar1=1.0 / 2048, scalar2=1e-6,
                                           op0=ALU.mult, op1=ALU.add), inc="ve")
            p.drain("v")
            rsqrt_cols(st_col(SC_A), st_col(SC_RSTD), st_col(SC_T1))
            p.w("v", "mm", None)
            v_op(lambda e, tt=tt: e.scalar_tensor_tensor(out=h2stg, in0=x1v[tt],
                                                         scalar=st_col(SC_RSTD),
                                                         in1=wln2b, op0=ALU.mult,
                                                         op1=ALU.mult), inc="ve")
            p.w("t", "ve", None)
            for hp in range(8):
                tb_key, tb = ("PT", PT) if hp % 2 == 0 else ("PT2", PT2)
                transpose_pair(h2stg[:, (2 * hp) * 128:(2 * hp + 1) * 128],
                               h2stg[:, (2 * hp + 1) * 128:(2 * hp + 2) * 128],
                               h2T[2 * hp][:, tt * 128:(tt + 1) * 128],
                               h2T[2 * hp + 1][:, tt * 128:(tt + 1) * 128], tb_key, tb)

        phase_guard(("g",))
        rtok = dma_g(wrouterb[:, 0:128], w_router_t)
        p.w("t", "sc", None)
        for tt in range(8):
            psk = tt % 2
            bank_wait("t", psk)
            p.w("t", *rtok)
            for ht in range(16):
                t_op(lambda e, ht=ht, tt=tt, psk=psk:
                     e.matmul(PS[psk][:, 0:8], h2T[ht][:, tt * 128:(tt + 1) * 128],
                              wrouterb[:, ht * 8:(ht + 1) * 8],
                              start=(ht == 0), stop=(ht == 15)), inc="mm")
            p.w("v", "mm", None)
            v_op(lambda e, psk=psk: e.reduce_max(st_col(SC_TMP), PS[psk][:, 0:8],
                                                 axis=AX.X, negate=True), inc="ve")
            p.drain("v")
            p.w("s", "ve", None)
            s_op(lambda e, psk=psk: e.activation(out=stat[:, 40:48], in_=PS[psk][:, 0:8],
                                                 func=AF.Exp, bias=st_col(SC_TMP),
                                                 scale=1.0), inc="sc")
            bank_rel(psk, "sc")
            p.w("v", "sc", None)
            v_op(lambda e: e.reduce_sum(st_col(SC_TMP), stat[:, 40:48], axis=AX.X))
            p.drain("v")
            v_op(lambda e: e.reciprocal(st_col(SC_A), st_col(SC_TMP)))
            p.drain("v")
            v_op(lambda e: e.tensor_scalar(out=stat[:, 40:48], in0=stat[:, 40:48],
                                           scalar1=st_col(SC_A), scalar2=None,
                                           op0=ALU.mult))
            p.drain("v")
            v_op(lambda e: e.reduce_max(st_col(SC_TMP), stat[:, 40:48], axis=AX.X))
            p.drain("v")
            v_op(lambda e: e.tensor_scalar(out=stat[:, 48:56], in0=stat[:, 40:48],
                                           scalar1=st_col(SC_TMP), scalar2=None,
                                           op0=ALU.is_equal))
            p.drain("v")
            v_op(lambda e: e.tensor_scalar(out=stat[:, 48:56], in0=stat[:, 48:56],
                                           scalar1=-1.0, scalar2=1.0,
                                           op0=ALU.mult, op1=ALU.add))
            p.drain("v")
            v_op(lambda e: e.tensor_tensor(stat[:, 48:56], stat[:, 40:48], stat[:, 48:56],
                                           ALU.mult))
            p.drain("v")
            v_op(lambda e: e.reduce_max(st_col(SC_TMP), stat[:, 48:56], axis=AX.X))
            p.drain("v")
            v_op(lambda e: e.tensor_scalar(out=stat[:, 48:56], in0=stat[:, 40:48],
                                           scalar1=st_col(SC_TMP), scalar2=None,
                                           op0=ALU.is_ge))
            p.drain("v")
            v_op(lambda e, tt=tt: e.tensor_tensor(comb[:, tt * 8:(tt + 1) * 8],
                                                  stat[:, 40:48], stat[:, 48:56],
                                                  ALU.mult), inc="ve")
            p.drain("v")
        bank_rel("shegb", "mm")
        phase_guard(("g",))

        wgbuf = [LAR.v(0, 2048, BF16), LAR.v(4096, 2048, BF16)]
        wubuf = [LAR.v(8192, 2048, BF16), LAR.v(12288, 2048, BF16)]
        wdbuf = [LAR.v(16384, 5632, BF16), LAR.v(27648, 5632, BF16)]
        cbrow = LAR.v(38912, 1024, F32)
        cbhf = LAR.v(43008, 1024, F32)
        cbhb = LAR.v(47104, 1024, BF16)
        cblb = LAR.v(49152, 1024, BF16)
        cbb = LAR.v(51200, 1024, BF16)
        gsil = LAR.v(43008, 512, BF16)
        gtmp = LAR.v(44032, 512, BF16)

        def run_expert(gsrc, usrc, dsrc, nit):
            strips = [(i, 1) for i in range(nit)]
            p.w("v", "mm", None)
            for si, (i0, iw) in enumerate(strips):
                wtoks = []
                for (bufs, srcf) in ((wgbuf, gsrc), (wubuf, usrc)):
                    key = f"{'wg' if bufs is wgbuf else 'wu'}{si % 2}"
                    bank_wait("g", key)
                    wtoks.append(dma_g(bufs[si % 2][:, 0:16 * iw * 128], srcf(i0, iw)))
                for li in range(iw):
                    it = i0 + li
                    for half in range(2):
                        pg, pu = half, 2 + half
                        bank_wait("t", pg)
                        bank_wait("t", pu)
                        p.w("t", *wtoks[0])
                        p.w("t", *wtoks[1])
                        for ht in range(16):
                            t_op(lambda e, ht=ht, li=li, half=half, si=si, iw=iw, pg=pg:
                                 e.matmul(PS[pg][:, 0:512],
                                          wgbuf[si % 2][:, ht * iw * 128 + li * 128:
                                                        ht * iw * 128 + (li + 1) * 128],
                                          h2T[ht][:, half * 512:(half + 1) * 512],
                                          start=(ht == 0), stop=(ht == 15)))
                        for ht in range(16):
                            t_op(lambda e, ht=ht, li=li, half=half, si=si, iw=iw, pu=pu:
                                 e.matmul(PS[pu][:, 0:512],
                                          wubuf[si % 2][:, ht * iw * 128 + li * 128:
                                                        ht * iw * 128 + (li + 1) * 128],
                                          h2T[ht][:, half * 512:(half + 1) * 512],
                                          start=(ht == 0), stop=(ht == 15)),
                                 inc="mm" if ht == 15 else None)
                        p.w("s", "mm", None)
                        p.w("s", "ve", None)
                        s_op(lambda e, pg=pg: e.activation(out=gsil[:, 0:512],
                                                           in_=PS[pg][:, 0:512],
                                                           func=AF.Silu), inc="sc")
                        bank_rel(pg, "sc")
                        p.w("v", "sc", None)
                        v_op(lambda e, pu=pu: e.tensor_tensor(gtmp[:, 0:512],
                                                              PS[pu][:, 0:512],
                                                              gsil[:, 0:512], ALU.mult))
                        p.drain("v")
                        v_op(lambda e, it=it, half=half:
                             e.tensor_tensor(guv[it][:, half * 512:(half + 1) * 512],
                                             gtmp[:, 0:512],
                                             cbb[:, half * 512:(half + 1) * 512],
                                             ALU.mult), inc="ve")
                        bank_rel(pu, "ve")
                bank_rel(f"wg{si % 2}", "mm")
                bank_rel(f"wu{si % 2}", "mm")
            ve_gu = p.c("ve")
            for js in range(4):
                key = f"wd{js % 2}"
                bank_wait("g", key)
                dtok = dma_g(wdbuf[js % 2][:, 0:nit * 512], dsrc(js))
                p.w("t", "ve", ve_gu)
                for tt in range(8):
                    psk = tt % 2
                    bank_wait("t", psk)
                    p.w("t", *dtok)
                    for it in range(nit):
                        t_op(lambda e, it=it, tt=tt, js=js, psk=psk:
                             e.matmul(PS[psk][:, 0:512], guv[it][:, tt * 128:(tt + 1) * 128],
                                      wdbuf[js % 2][:, it * 512:(it + 1) * 512],
                                      start=(it == 0), stop=(it == nit - 1)),
                             inc="mm" if it == nit - 1 else None)
                    p.w("v", "mm", None)
                    v_op(lambda e, tt=tt, js=js, psk=psk:
                         e.tensor_tensor(x1v[tt][:, js * 512:(js + 1) * 512],
                                         x1v[tt][:, js * 512:(js + 1) * 512],
                                         PS[psk][:, 0:512], ALU.add), inc="ve")
                    bank_rel(psk, "ve")
                bank_rel(key, "mm")

        def build_cbb_from_statcols(col_of_tt):
            for tt in range(8):
                bank_wait("t", 4)
                p.w("t", "ve", None)
                t_op(lambda e, tt=tt: e.transpose(PS[4][0:1, 0:128], col_of_tt(tt), identf),
                     inc="mm")
                p.w("s", "mm", None)
                s_op(lambda e, tt=tt: e.copy(out=cbrow[0:1, tt * 128:(tt + 1) * 128],
                                             in_=PS[4][0:1, 0:128]), inc="sc")
                bank_rel(4, "sc")
            p.w("v", "sc", None)
            v_op(lambda e: e.engine_nop(), inc="ve")
            bcast_hilo(cbrow, 1024, cbhf, cbhb, cblb, cbb)
            p.w("v", "sc", None)
            v_op(lambda e: e.engine_nop(), inc="ve")

        for ex in range(N_EXP):
            build_cbb_from_statcols(lambda tt, ex=ex: comb[:, tt * 8 + ex: tt * 8 + ex + 1])
            run_expert(
                lambda i0, iw, ex=ex: w_gate_t[ex, i0],
                lambda i0, iw, ex=ex: w_up_t[ex, i0],
                lambda js, ex=ex: w_down_t[ex, js],
                11)

        shegb = LAR.v(45056, 16, BF16)
        bank_wait("g", "shegb")
        egtok = dma_g(shegb[:, 0:16], w_sheg_t)
        bank_wait("t", 0)
        p.w("t", *egtok)
        for tt in range(8):
            for ht in range(16):
                t_op(lambda e, ht=ht, tt=tt:
                     e.matmul(PS[0][:, tt:tt + 1], h2T[ht][:, tt * 128:(tt + 1) * 128],
                              shegb[:, ht:ht + 1], start=(ht == 0), stop=(ht == 15)),
                     inc="mm" if (ht == 15 and tt == 7) else None)
        p.w("s", "mm", None)
        s_op(lambda e: e.activation(out=stat[:, 56:64], in_=PS[0][:, 0:8],
                                    func=AF.Sigmoid), inc="sc")
        bank_rel(0, "sc")
        p.w("v", "sc", None)
        v_op(lambda e: e.engine_nop(), inc="ve")
        build_cbb_from_statcols(lambda tt: stat[:, 56 + tt:57 + tt])
        for chk in range(4):
            run_expert(
                lambda i0, iw, chk=chk: w_shg_t[chk * 11 + i0],
                lambda i0, iw, chk=chk: w_shu_t[chk * 11 + i0],
                lambda js, chk=chk: w_shd_t[chk * 4 + js],
                11)
        emit_taps("moe")

    if last_idx >= 5:
        phase_guard()
        gt = [LAR.v(0, 2048, F32), LAR.v(8192, 2048, F32)]
        xt2 = LAR.v(16384, 2048, F32)
        st2 = [LAR.v(24576, 2048, F32), LAR.v(32768, 2048, F32)]
        for tt in range(8):
            bank_wait("y", f"st2{tt % 2}")
            dma_y(st2[tt % 2], s_spill[tt * 128:(tt + 1) * 128, :])
            bank_wait("y", "xt2")
            dma_y(xt2, x_ownd[tt * 128:(tt + 1) * 128, :])
            bank_wait("s", f"gt{tt % 2}")
            p.w("s", "ve", None)
            s_op(lambda e, tt=tt: e.activation(out=gt[tt % 2], in_=x1v[tt], func=AF.Silu),
                 inc="sc")
            p.w("v", "sc", None)
            w_all("v", "xl")
            v_op(lambda e, tt=tt: e.tensor_tensor(gt[tt % 2], st2[tt % 2], gt[tt % 2],
                                                  ALU.mult))
            bank_rel(f"st2{tt % 2}", "ve")
            p.drain("v")
            v_op(lambda e, tt=tt: e.tensor_tensor(gt[tt % 2], gt[tt % 2], xt2, ALU.add),
                 inc="ve")
            bank_rel("xt2", "ve")
            p.w("y", "ve", None)
            ytok = store_y(y_out[tt * 128:(tt + 1) * 128, :], gt[tt % 2])
            bank_rel_t(f"gt{tt % 2}", ytok)
        emit_taps("final")

    w_all("y", "st")
    w_all("y", "sg")
    w_all("g", "st")
    w_all("g", "sg")

    sem_names = (["mm", "ve", "sc"] + [f"xl{i}" for i in range(3)]
                 + [f"wl{i}" for i in range(4)] + [f"st{i}" for i in range(2)]
                 + [f"sg{i}" for i in range(2)])
    with contextlib.ExitStack() as stk:
        sems = {n: stk.enter_context(nc.semaphore(n)) for n in sem_names}
        blk = stk.enter_context(nc.Block())
        for name, deco in (("t", blk.tensor), ("v", blk.vector), ("s", blk.scalar),
                           ("g", blk.gpsimd), ("y", blk.sync)):
            deco(lambda eng, name=name: p.emit_engine(name, eng, sems))
    return nc


def make_host_constants():
    ident = np.eye(128, dtype=np.float32)
    sperm = np.zeros((128, 128), np.float32)
    for dp in range(64):
        sperm[dp + 64, dp] = -1.0
    for dp in range(64, 128):
        sperm[dp - 64, dp] = 1.0
    import jax
    import jax.numpy as jnp
    with jax.default_device(jax.devices("cpu")[0]):
        invj = 1.0 / 1.0e6 ** (jnp.arange(0, 128, 2, dtype=jnp.float32) / 128)
        freqs = jnp.arange(S, dtype=jnp.float32)[:, None] * invj[None, :]
        embj = jnp.concatenate([freqs, freqs], axis=-1)
        cosj = np.asarray(jnp.cos(embj)).T
        sinj = np.asarray(jnp.sin(embj)).T
    zeros = np.zeros_like(cosj)
    ctabk = np.stack([cosj, zeros, sinj, zeros]).astype(np.float32)
    cos64, sin64 = cosj.astype(np.float64), sinj.astype(np.float64)
    return ident, sperm, ctabk, cos64, sin64


def make_masks(j):
    tri = np.where(np.arange(128)[:, None] > np.arange(128)[None, :], NEGBIG, 0.0)
    full = np.full((128, 128), NEGBIG)
    zero = np.zeros((128, 128))
    if j == 0:
        m0, m1 = tri, full
    else:
        m0, m1 = zero, tri
    return np.concatenate([m0, m1], 1).astype(np.float32)


def make_in_maps(inputs):
    x = np.ascontiguousarray(np.asarray(inputs["L0_attn_hidden_states"], np.float32))
    wqkvT = np.ascontiguousarray(np.asarray(inputs["L0_attn_Wqkv_w"], np.float32).T)
    outwT = np.ascontiguousarray(np.asarray(inputs["L0_attn_out_w"], np.float32).T)
    bqkv3 = np.ascontiguousarray(np.stack([
        np.asarray(inputs["L0_ffn_bq"], np.float32),
        np.asarray(inputs["L0_ffn_bk"], np.float32),
        np.asarray(inputs["L0_ffn_bv"], np.float32)]))
    lamv = np.concatenate([
        np.asarray(inputs["L0_attn_lambda_q1"], np.float32),
        np.asarray(inputs["L0_attn_lambda_k1"], np.float32),
        np.asarray(inputs["L0_attn_lambda_q2"], np.float32),
        np.asarray(inputs["L0_attn_lambda_k2"], np.float32)])
    ident, sperm, ctabk, cos64, sin64 = make_host_constants()

    A = np.ascontiguousarray

    def tile_in_out(w, cw):
        nin, nout = w.shape
        return A(w.reshape(nin // 128, 128, nout // cw, cw).transpose(2, 1, 0, 3)
                 .reshape(nout // cw, 128, (nin // 128) * cw))

    wq32 = np.asarray(inputs["L0_ffn_wq"], np.float32)
    wk32 = np.asarray(inputs["L0_ffn_wk"], np.float32)
    wv32 = np.asarray(inputs["L0_ffn_wv"], np.float32)
    wo32 = np.asarray(inputs["L0_ffn_wo"], np.float32)
    wg32 = np.asarray(inputs["L0_ffn_w_gate_e"], np.float32)
    wu32 = np.asarray(inputs["L0_ffn_w_up_e"], np.float32)
    wd32 = np.asarray(inputs["L0_ffn_w_down_e"], np.float32)
    shg32 = np.asarray(inputs["L0_ffn_w_sh_gate"], np.float32)
    shu32 = np.asarray(inputs["L0_ffn_w_sh_up"], np.float32)
    shd32 = np.asarray(inputs["L0_ffn_w_sh_down"], np.float32)

    common = {
        "wqkv_qk": tile_in_out(wqkvT[:, 0:3072], 128),
        "wqkv_v": tile_in_out(wqkvT[:, 3072:4096], 256),
        "wqkv_b": np.asarray(inputs["L0_attn_Wqkv_b"], np.float32),
        "outwT_t": tile_in_out(outwT, 256),
        "outb": np.asarray(inputs["L0_attn_out_b"], np.float32),
        "lamv": lamv,
        "sublnw": np.asarray(inputs["L0_attn_subln_w"], np.float32),
        "w_ln1": np.asarray(inputs["L0_ffn_w_ln1"], np.float32),
        "wq_t": tile_in_out(wq32, 128),
        "wk_t": tile_in_out(wk32, 128),
        "wv_t": tile_in_out(wv32, 128),
        "bqkv3": bqkv3,
        "wo_t": tile_in_out(wo32, 512),
        "w_ln2": np.asarray(inputs["L0_ffn_w_ln2"], np.float32),
        "w_router_t": A(np.asarray(inputs["L0_ffn_w_router"], np.float32)
                        .reshape(16, 128, 8).transpose(1, 0, 2).reshape(128, 128)),
        "w_gate_t": np.stack([tile_in_out(wg32[e], 128) for e in range(N_EXP)]),
        "w_up_t": np.stack([tile_in_out(wu32[e], 128) for e in range(N_EXP)]),
        "w_down_t": np.stack([tile_in_out(wd32[e], 512) for e in range(N_EXP)]),
        "w_shg_t": tile_in_out(shg32, 128),
        "w_shu_t": tile_in_out(shu32, 128),
        "w_shd_t": np.stack([tile_in_out(shd32[c * MOE_I:(c + 1) * MOE_I, :], 512)
                             for c in range(4)]).reshape(16, 128, 11 * 512),
        "w_sheg_t": A(np.asarray(inputs["L0_ffn_w_sh_eg"], np.float32)
                      .reshape(16, 128).T),
        "ident": ident,
        "sperm": sperm,
        "ctabk": ctabk,
    }

    in_maps = []
    for c in range(8):
        b, j = c // 2, c % 2
        rows = own_rows(j)
        cq64, sq64 = cos64[:, rows], sin64[:, rows]

        def hilo(a64):
            hi = a64.astype(np.float32)
            lo = (a64 - hi.astype(np.float64)).astype(np.float32)
            return hi, lo

        cq_hi, cq_lo = hilo(cq64)
        sq_hi, sq_lo = hilo(sq64)
        m = dict(common)
        m["x_all"] = np.ascontiguousarray(x[b])
        m["x_own"] = np.ascontiguousarray(x[b][rows])
        m["masks"] = make_masks(j)
        m["ctabq"] = np.stack([cq_hi, cq_lo, sq_hi, sq_lo]).astype(np.float32)
        in_maps.append(m)
    return in_maps


def kernel(**inputs):
    in_maps = make_in_maps(inputs)
    nc = build_nc()
    res = run_bass_kernel_spmd(nc, in_maps, core_ids=list(range(8)))
    out = np.zeros((B, S, HID), np.float32)
    for c in range(8):
        out[c // 2, own_rows(c % 2)] = res.results[c]["y"]
    return out
